# version 31 (final)
# baseline (speedup 1.0000x reference)
import numpy as np
import ml_dtypes
from contextlib import ExitStack
import concourse.bass as bass
import concourse.mybir as mybir
from concourse.bass_utils import run_bass_kernel_spmd

F32 = mybir.dt.float32
BF16 = mybir.dt.bfloat16
U8 = mybir.dt.uint8
AF = mybir.ActivationFunctionType
ALU = mybir.AluOpType
AX = mybir.AxisListType.X

P = 128
C = 1024
CCH = 8
T = 2048
NG = 16
NGH = 17
LCH = 64
NHEAD = 16
HK = 64
W = 512
FH = 8
FD = 64
INNER = 512
HID = 2730
HIDP = 2816
NEG = -1.0e30

BF = ml_dtypes.bfloat16


class Bld:

    ENGS = ("pe", "dve", "act", "pool", "sp")
    SEMS = ENGS + ("pld", "cc")

    def __init__(self, nc, es):
        self.nc = nc
        self.eng = dict(pe=nc.tensor, dve=nc.vector, act=nc.scalar,
                        pool=nc.gpsimd, sp=nc.sync)
        self.sem = {e: es.enter_context(nc.semaphore(f"sem_{e}")) for e in self.SEMS}
        self.cnt = {e: 0 for e in self.SEMS}
        self.lastw = {e: {e2: 0 for e2 in self.SEMS} for e in self.ENGS}

    def mark(self, e, ins, k=1):
        if e == "pool" and k == 16:
            e = "pld"
        ins.then_inc(self.sem[e], k)
        self.cnt[e] += k
        return (e, self.cnt[e])

    def ms(self, e):
        if e == "pool":
            return [("pool", self.cnt["pool"]), ("pld", self.cnt["pld"])]
        return (e, self.cnt[e])

    def wait(self, e, *mss):
        flat = []
        for m in mss:
            if isinstance(m, list):
                flat.extend(m)
            elif m is not None:
                flat.append(m)
        for (e2, v) in flat:
            if v <= 0:
                continue
            if self.lastw[e][e2] < v:
                self.eng[e].wait_ge(self.sem[e2], v)
                self.lastw[e][e2] = v

    def fence(self, e, ins):
        m = self.mark(e, ins)
        self.wait(e, m)
        return m

    def barrier(self, engs=None):
        engs = engs or self.ENGS
        mss = [self.ms(e) for e in engs]
        for e in engs:
            self.wait(e, *mss)


def _rotary_planes(positions):
    n = len(positions)
    half_idx = np.arange(0, FD, 2, dtype=np.float64)
    inv_freq = 1.0 / 10000.0 ** (half_idx / FD)
    t = positions.astype(np.float64)
    freqs = t[:, None] * inv_freq[None, :]
    freqs = np.concatenate([freqs, freqs], -1)
    base = (half_idx + 0.4 * FD) / (1.4 * FD)
    power = (t - 4096 // 2) / float(W // 2)
    scale = base[None, :] ** power[:, None]
    scale = np.concatenate([scale, scale], -1)
    cos, sin = np.cos(freqs), np.sin(freqs)
    sgn = np.where(np.arange(FD) < 32, -1.0, 1.0)[None, :]
    return (np.ascontiguousarray((8.0 * cos * scale), np.float32),
            np.ascontiguousarray((8.0 * sin * scale * sgn), np.float32),
            np.ascontiguousarray((cos / scale), np.float32),
            np.ascontiguousarray((sin / scale * sgn), np.float32))


def make_core_inputs(inputs, core):
    b, half = core // 2, core % 2
    x = np.ascontiguousarray(np.asarray(inputs["L0_attn_x"], np.float32))
    d = {}

    def halo_block(row127):
        blk = np.zeros((P, C), np.float32)
        if row127 is not None:
            blk[127] = row127
        return blk

    t0 = half * T
    d["xm"] = np.ascontiguousarray(np.concatenate(
        [halo_block(x[b, t0 - 1] if t0 > 0 else None), x[b, t0:t0 + T]], 0))
    if half == 1:
        d["xp"] = np.ascontiguousarray(
            np.concatenate([halo_block(None), x[b, 0:T]], 0))
    else:
        d["xp"] = np.zeros((P + T, C), np.float32)

    for nm in ("Wr", "Wk", "Wv", "Wg", "Wo", "tm_w1", "td_w1"):
        d[nm] = np.ascontiguousarray(np.asarray(inputs["L0_attn_" + nm]))
    d["tm_w2a"] = np.ascontiguousarray(np.concatenate(
        [np.asarray(inputs["L0_attn_tm_w2"]),
         np.asarray(inputs["L0_attn_time_maa_wkvrg"])[:, None, :]], 1))
    d["td_w2a"] = np.ascontiguousarray(np.concatenate(
        [np.asarray(inputs["L0_attn_td_w2"]),
         np.asarray(inputs["L0_attn_time_decay"])[None, :]], 0))
    d["tmx"] = np.ascontiguousarray(
        np.asarray(inputs["L0_attn_time_maa_x"]).reshape(CCH, P).T)
    u = np.asarray(inputs["L0_attn_time_faaaa"]).astype(np.float32)
    ufm = np.zeros((P, CCH), np.float32)
    for h in range(NHEAD):
        ufm[(h % 2) * 64:(h % 2) * 64 + 64, h // 2] = u[h]
    d["u_fm"] = ufm
    d["glnw"] = np.ascontiguousarray(np.broadcast_to(
        np.asarray(inputs["L0_attn_ln_w"], np.float32), (P, C)))
    d["glnb"] = np.ascontiguousarray(np.broadcast_to(
        np.asarray(inputs["L0_attn_ln_b"], np.float32), (P, C)))

    fw = {kk[len("L0_ffn_"):]: np.asarray(v) for kk, v in inputs.items()
          if kk.startswith("L0_ffn_")}
    for l in (0, 1):
        d[f"wqkv{l}"] = np.ascontiguousarray(fw["wqkv"][l])
        d[f"wo{l}"] = np.ascontiguousarray(fw["wo"][l])
        w1 = fw["w1"][l]
        w1p = np.zeros((C, 2 * HIDP), np.float32)
        w1p[:, :HID] = w1[:, :HID]
        w1p[:, HIDP:HIDP + HID] = w1[:, HID:]
        d[f"w1{l}"] = w1p
        b1 = np.asarray(fw["b1"][l], np.float32)
        b1p = np.zeros((2 * HIDP,), np.float32)
        b1p[:HID] = b1[:HID]
        b1p[HIDP:HIDP + HID] = b1[HID:]
        d[f"b1a{l}"] = np.ascontiguousarray(b1p[:HIDP].reshape(HIDP // P, P).T)
        d[f"b1g{l}"] = np.ascontiguousarray(b1p[HIDP:].reshape(HIDP // P, P).T)
        w2 = fw["w2"][l]
        w2p = np.zeros((HIDP, C), np.float32)
        w2p[:HID] = w2
        d[f"w2{l}"] = w2p
        d[f"b2c{l}"] = np.ascontiguousarray(
            np.asarray(fw["b2"][l], np.float32).reshape(CCH, P).T)
        d[f"qs{l}"] = np.ascontiguousarray(np.broadcast_to(
            np.asarray(fw["q_scale"][l], np.float32), (P, FD)))
        d[f"ks{l}"] = np.ascontiguousarray(np.broadcast_to(
            np.asarray(fw["k_scale"][l], np.float32), (P, FD)))
        d[f"flnw{l}"] = np.ascontiguousarray(
            np.asarray(fw["ln_g"][l], np.float32).reshape(CCH, P).T)
        d[f"flnb{l}"] = np.ascontiguousarray(
            np.asarray(fw["ln_b"][l], np.float32).reshape(CCH, P).T)

    d["ident"] = np.eye(P, dtype=np.float32)
    d["eye64st"] = np.ascontiguousarray(np.tile(np.eye(64, dtype=np.float32), (2, 1)))
    o2 = np.zeros((P, 2), np.float32)
    o2[0:64, 0] = 1.0
    o2[64:128, 1] = 1.0
    d["ones2"] = o2
    d["scan_mask"] = (np.arange(P)[:, None] < np.arange(P)[None, :]).astype(np.uint8)
    d["diag_mask"] = np.where(
        np.arange(P)[:, None] >= np.arange(P)[None, :], 0.0, NEG
    ).astype(np.float32)
    d["mask0c"] = np.full((P, 1), 0.0 if half == 1 else NEG, np.float32)
    d["diag_maskT"] = np.where(
        np.arange(P)[:, None] <= np.arange(P)[None, :], 0.0, NEG
    ).astype(BF)
    d["prevmask"] = np.ascontiguousarray(np.where(
        np.arange(896)[None, :] > np.arange(P)[:, None] + 384,
        NEG, 0.0).astype(BF))
    d["halo_gate"] = np.full((P, 1), 1.0 if half == 1 else 0.0, np.float32)
    d["halo_gate_inv"] = np.full((P, 1), 0.0 if half == 1 else 1.0, np.float32)
    pos_main = np.arange(t0, t0 + T)
    d["rot_main"] = np.ascontiguousarray(np.stack(_rotary_planes(pos_main), 0))
    pos_halo = np.arange(W, T)
    d["rot_halo"] = np.ascontiguousarray(np.stack(_rotary_planes(pos_halo), 0))
    return d


class K:
    def __init__(self, dbg=()):
        self.dbg_names = set(dbg)
        self.nc = bass.Bass("TRN2", target_bir_lowering=False)
        self.es = ExitStack()
        self.b = Bld(self.nc, self.es)
        self.inp = {}
        self.out = {}
        self.sb = {}
        self.ps = {}
        self.tmp = {}

    def dram_in(self, name, shape, dtype=F32):
        if name not in self.inp:
            self.inp[name] = self.nc.dram_tensor(
                name, list(shape), dtype, kind="ExternalInput").ap()
        return self.inp[name]

    def dram_out(self, name, shape, dtype=F32):
        self.out[name] = self.nc.dram_tensor(
            name, list(shape), dtype, kind="ExternalOutput").ap()
        return self.out[name]

    def dram_tmp(self, name, shape, dtype=BF16):
        if name not in self.tmp:
            self.tmp[name] = self.nc.dram_tensor(
                name, list(shape), dtype, kind="Internal").ap()
        return self.tmp[name]

    def sbuf(self, name, shape, dtype=BF16):
        if name not in self.sb:
            if not hasattr(self, "_sb_namectr"):
                self._sb_namectr = {}
            n = self._sb_namectr.get(name, 0)
            self._sb_namectr[name] = n + 1
            uniq = name if n == 0 else f"{name}__{n}"
            cm = self.nc.sbuf_tensor(uniq, list(shape), dtype)
            self.sb[name] = cm.__enter__()
            if not hasattr(self, "_sb_cms"):
                self._sb_cms = {}
                self._sb_order = []
            self._sb_cms[name] = cm
            self._sb_order.append(name)
        return self.sb[name]

    def sbuf_free_from(self, watermark):
        if watermark not in self._sb_order:
            return
        idx = self._sb_order.index(watermark)
        for name in reversed(self._sb_order[idx:]):
            self._sb_cms.pop(name).__exit__(None, None, None)
            del self.sb[name]
        self._sb_order = self._sb_order[:idx]

    def psum(self, name, shape, dtype=F32):
        if name not in self.ps:
            self.ps[name] = self.es.enter_context(
                self.nc.psum_tensor(name, list(shape), dtype))
        return self.ps[name]

    def banks(self):
        if not hasattr(self, "_banks"):
            self._banks = [self.psum(f"bank{i}", [P, 512], F32) for i in range(7)]
            self._banks.append(self.psum("bank7", [P, 1024], BF16))
        return self._banks

    def constv(self, value):
        if not hasattr(self, "_consts"):
            self._consts = {}
        if value not in self._consts:
            t = self.sbuf(f"const_{len(self._consts)}", [P, 1], F32)
            self.b.mark("pool", self.nc.gpsimd.memset(t[:], float(value)))
            self._consts[value] = t
        return self._consts[value][:]

    def dbg_dump(self, name, ap_sbuf, shape, dtype=F32, dep=None):
        if name not in self.dbg_names:
            return
        o = self.dram_out("dbg_" + name, shape, F32)
        if dep:
            self.b.wait("pool", dep)
        self.b.mark("pool", self.nc.gpsimd.dma_start(out=o, in_=ap_sbuf), 16)

    def emit_ln_fm(self, x_dram, xn_fm_ap, ngroups):
        nc, b = self.nc, self.b
        xt = self.sbuf("scrF1", [P, C], F32)
        cen = self.sbuf("scrF2", [P, C], F32)
        xnb = self.sbuf("scrB1", [P, C], BF16)
        st = self.sbuf("ln_st", [P, 4], F32)
        tps = self.banks()[7]
        for g in range(ngroups):
            ld = b.mark("sp", nc.sync.dma_start(
                out=xt[:], in_=x_dram[g * P:(g + 1) * P, :]), 16)
            b.wait("dve", ld)
            m0 = b.mark("dve", nc.vector.reduce_sum(st[:, 0:1], xt[:], axis=AX))
            b.wait("dve", m0)
            m1 = b.mark("dve", nc.vector.tensor_scalar_mul(
                st[:, 1:2], st[:, 0:1], 1.0 / C))
            b.wait("dve", m1)
            nc.vector.tensor_scalar(cen[:], xt[:], st[:, 1:2], None, ALU.subtract)
            sq = xt
            nc.vector.tensor_mul(sq[:], cen[:], cen[:])
            m1b = b.mark("dve", nc.vector.reduce_sum(st[:, 2:3], sq[:], axis=AX))
            b.wait("dve", m1b)
            m2 = b.mark("dve", nc.vector.tensor_scalar_mul(
                st[:, 2:3], st[:, 2:3], 1.0 / C))
            b.wait("act", m2)
            eps_ap = self.constv(1e-5)
            if g == 0:
                b.wait("act", self.b.ms("pool"))
            nc.scalar.activation(st[:, 3:4], st[:, 2:3], AF.Ln, bias=eps_ap)
            m3 = b.mark("act", nc.scalar.activation(
                st[:, 3:4], st[:, 3:4], AF.Exp, scale=-0.5))
            b.wait("dve", m3)
            m4 = b.mark("dve", nc.vector.tensor_scalar(
                xnb[:], cen[:], st[:, 3:4], None, ALU.mult))
            b.wait("pe", m4)
            mt = None
            for cc in range(CCH):
                mt = nc.tensor.transpose(tps[:, cc * P:(cc + 1) * P],
                                         xnb[:, cc * P:(cc + 1) * P],
                                         self.sb["ident_bf"][:])
            mt = b.mark("pe", mt)
            b.wait("dve", mt)
            ev = b.mark("dve", nc.vector.tensor_copy(
                xn_fm_ap[:, :, g * P:(g + 1) * P],
                tps.ap().rearrange("p (c n) -> p c n", c=CCH)))
            if g == 3:
                self.dbg_dump("ln_xt", xt[:], [P, C], dep=ld)
                self.dbg_dump("ln_st", st[:], [P, 4], dep=b.ms("act"))
                self.dbg_dump("ln_xnb", xnb[:], [P, C], dep=m4)
            b.wait("sp", ev)
            b.wait("pe", ev)
        return b.ms("dve")


    def load_w_bf(self, bufname, dram_ap, rows, cols):
        kc = rows // P
        sb = self.sbuf(bufname, [P, kc, cols], BF16)
        m = self.b.mark("pool", self.nc.gpsimd.dma_start(
            out=sb.ap()[:, 0:kc, 0:cols],
            in_=dram_ap.rearrange("(k p) n -> p k n", p=P)), 16)
        return sb, m

    def emit_rwkv_mix(self, xn_fm, which, T_, tag):
        nc, b = self.nc, self.b
        ntile = T_ // 512
        IW = {"w": 0, "k": 1, "v": 2, "r": 3, "g": 4}
        tm1, m1ld = self.load_w_bf("wsmall", self.dram_in("tm_w1", [C, 160]), C, 160)
        tm2 = self.sbuf("w_tm2", [33, 5, C], BF16)
        m2ld = b.mark("pool", nc.gpsimd.dma_start(
            out=tm2.ap(), in_=self.dram_in("tm_w2a", [5, 33, C]).rearrange("f a n -> a f n")), 16)
        td1 = self.sbuf("w_td1", [P, CCH, 64], BF16)
        td1ld = b.mark("pool", nc.gpsimd.dma_start(
            out=td1.ap(), in_=self.dram_in("td_w1", [C, 64]).rearrange("(k p) n -> p k n", p=P)), 16)
        td2 = self.sbuf("w_td2", [65, C], BF16)
        td2ld = b.mark("pool", nc.gpsimd.dma_start(
            out=td2.ap(), in_=self.dram_in("td_w2a", [65, C])), 16)
        tmx = self.sbuf("tmx_sb", [P, CCH], F32)
        tmxld = b.mark("pool", nc.gpsimd.dma_start(
            out=tmx.ap(), in_=self.dram_in("tmx", [P, CCH])), 16)

        sx = self.sbuf("mx_sx", [P, CCH, 512], BF16)
        xxx = self.sbuf("mx_xxx", [P, CCH, 512], BF16)
        xi = self.sbuf("mx_xi", [P, CCH, 512], BF16)
        evb = self.sbuf("mx_ev", [P, 512], BF16)
        evb2 = self.sbuf("mx_ev2", [P, 512], BF16)
        mts = [self.sbuf(f"mx_m{f}", [33, T], BF16) for f in range(5)]
        m2a = self.sbuf("mx_m2a", [65, T], BF16)
        B = self.banks()
        psm = B[2][0:32, :]
        psd = B[2][:, :]
        ps2 = B[2][0:64, :]
        pso = B[0][:, :]
        pso2 = B[1][:, :]

        for f in range(5):
            b.mark("pool", nc.gpsimd.memset(mts[f][32:33, 0:T_], 1.0))
        b.mark("pool", nc.gpsimd.memset(m2a[64:65, 0:T_], 1.0))
        ones_ms = b.ms("pool")

        def emit_sx(tt):
            c0 = P + tt * 512
            for cc in range(CCH):
                m = nc.vector.tensor_sub(sx[:, cc, :],
                                         xn_fm.ap()[:, cc, c0 - 1:c0 + 511],
                                         xn_fm.ap()[:, cc, c0:c0 + 512])
            return b.mark("dve", m)

        need_f = sorted({IW[i] for i in which})
        b.wait("pe", m1ld, m2ld, tmxld, ones_ms)
        b.wait("dve", tmxld)
        for tt in range(ntile):
            c0 = P + tt * 512
            b.wait("dve", b.ms("pe"))
            sxm = emit_sx(tt)
            for cc in range(CCH):
                nc.vector.tensor_scalar(xxx[:, cc, :], sx[:, cc, :],
                                        tmx[:, cc:cc + 1], None, ALU.mult)
                m = nc.vector.tensor_add(xxx[:, cc, :], xxx[:, cc, :],
                                         xn_fm.ap()[:, cc, c0:c0 + 512])
            mx = b.mark("dve", m)
            b.wait("pe", mx)
            for f in need_f:
                for kc in range(CCH):
                    mm = nc.tensor.matmul(psm, tm1[:, kc, 32 * f:32 * f + 32],
                                          xxx.ap().rearrange("p c n -> p (c n)")[:, kc * 512:(kc + 1) * 512],
                                          start=(kc == 0), stop=(kc == CCH - 1))
                mms = b.mark("pe", mm)
                b.wait("act", mms)
                ev = b.mark("act", nc.scalar.activation(
                    mts[f][0:32, tt * 512:(tt + 1) * 512], psm, AF.Tanh))
                b.wait("pe", ev)

        def fold_tile(i, tt):
            f = IW[i]
            c0 = P + tt * 512
            b.wait("dve", b.ms("pe"))
            b.wait("pool", b.ms("pe"))
            sxm = emit_sx(tt)
            for cc in range(CCH):
                mm = nc.tensor.matmul(psd, tm2[:, f, cc * P:(cc + 1) * P],
                                      mts[f][:, tt * 512:(tt + 1) * 512],
                                      start=True, stop=True)
                mms = b.mark("pe", mm)
                b.wait("dve", mms, sxm)
                q = b.mark("dve", nc.vector.tensor_mul(
                    xi[:, cc, :], psd, sx[:, cc, :]))
                b.wait("pe", q)
                b.wait("pool", q)
                b.mark("pool", nc.gpsimd.tensor_add(
                    xi[:, cc, :], xi[:, cc, :], xn_fm.ap()[:, cc, c0:c0 + 512]))
            return b.ms("pool")

        for i in which:
            if i == "w":
                b.wait("pe", td1ld, td2ld)
                for tt in range(ntile):
                    fm = fold_tile(i, tt)
                    b.wait("pe", fm)
                    for kc in range(CCH):
                        mm = nc.tensor.matmul(ps2, td1[:, kc, :],
                                              xi.ap().rearrange("p c n -> p (c n)")[:, kc * 512:(kc + 1) * 512],
                                              start=(kc == 0), stop=(kc == CCH - 1))
                    mms = b.mark("pe", mm)
                    b.wait("act", mms)
                    ev = b.mark("act", nc.scalar.activation(
                        m2a[0:64, tt * 512:(tt + 1) * 512], ps2, AF.Tanh))
                    b.wait("pe", ev)
            elif i in ("k", "r"):
                b.wait("pool", b.ms("pe"))
                wsb, wld = self.load_w_bf("wbufA" if i == "k" else "wbufA",
                                          self.dram_in("W" + i, [C, C]), C, C)
                dst = self.dram_tmp(i + "_sc", [CCH, P, T], BF16)
                b.wait("pe", wld)
                PS4 = [pso, pso2, B[3][:, :], B[4][:, :]]
                mxev = [None] * 4
                ebr = [None, None]
                rrm = 0
                for tt in range(ntile):
                    fm = fold_tile(i, tt)
                    b.wait("pe", fm)
                    for oc in range(CCH):
                        ii = rrm % 4
                        rrm += 1
                        ps = PS4[ii]
                        eb = evb if oc % 2 == 0 else evb2
                        if mxev[ii] is not None:
                            b.wait("pe", mxev[ii])
                        for kc in range(CCH):
                            mm = nc.tensor.matmul(
                                ps, wsb[:, kc, oc * P:(oc + 1) * P],
                                xi.ap().rearrange("p c n -> p (c n)")[:, kc * 512:(kc + 1) * 512],
                                start=(kc == 0), stop=(kc == CCH - 1))
                        mms = b.mark("pe", mm)
                        b.wait("dve", mms)
                        if ebr[oc % 2] is not None:
                            b.wait("dve", ebr[oc % 2])
                        ev = b.mark("dve", nc.vector.tensor_copy(eb[:], ps))
                        mxev[ii] = ev
                        b.wait("sp", ev)
                        ebr[oc % 2] = b.mark("sp", nc.sync.dma_start(
                            out=dst[oc, :, tt * 512:(tt + 1) * 512], in_=eb[:]), 16)
            elif i in ("v", "g"):
                b.wait("pool", b.ms("pe"))
                wsb, wld = self.load_w_bf("wbufB",
                                          self.dram_in("W" + i, [C, C]), C, C)
                dst = self.dram_tmp(i + "_sc", [T // P, P, C], BF16)
                act = AF.Silu if i == "g" else None
                b.wait("pe", wld)
                PS4 = [pso, pso2, B[3][:, :], B[4][:, :]]
                mxev = [None] * 4
                ebr = [None, None]
                rrm = 0
                eng = "act" if act else "dve"
                for tt in range(ntile):
                    fm = fold_tile(i, tt)
                    b.wait("pe", fm)
                    for gg in range(4):
                        grp = tt * 4 + gg
                        for half in range(2):
                            ii = rrm % 4
                            rrm += 1
                            ps = PS4[ii]
                            eb = evb if half == 0 else evb2
                            if mxev[ii] is not None:
                                b.wait("pe", mxev[ii])
                            for kc in range(CCH):
                                mm = nc.tensor.matmul(
                                    ps, xi[:, kc, gg * P:(gg + 1) * P],
                                    wsb[:, kc, half * 512:(half + 1) * 512],
                                    start=(kc == 0), stop=(kc == CCH - 1))
                            mms = b.mark("pe", mm)
                            b.wait(eng, mms)
                            if ebr[half] is not None:
                                b.wait(eng, ebr[half])
                            if act:
                                ev = b.mark("act", nc.scalar.activation(eb[:], ps, act))
                            else:
                                ev = b.mark("dve", nc.vector.tensor_copy(eb[:], ps))
                            mxev[ii] = ev
                            b.wait("sp", ev)
                            ebr[half] = b.mark("sp", nc.sync.dma_start(
                                out=dst[grp, :, half * 512:(half + 1) * 512],
                                in_=eb[:]), 16)
        b.barrier()
        return m2a, td2

    def emit_rwkv_scan(self, tag, T_, m2a, td2, S_init, with_y, save_corr=False):
        nc, b = self.nc, self.b
        import os
        SC_HP = int(os.environ.get("SC_HP", CCH))
        SC_CP = int(os.environ.get("SC_CP", 99))
        LC = P
        NCH = T_ // LC
        et = self.sbuf("sc_e", [P, 2048], F32)
        cums = self.sbuf("sc_cums", [P, 2048 + 1], F32)
        bst = self.sbuf("sc_bst", [P, 17], F32)
        stat = self.sbuf("sc_stat", [P, 3 * 16], F32)
        ksl = self.sbuf("sc_ksl", [P, 2048], BF16)
        rsl = self.sbuf("sc_rsl", [P, 2048], BF16)
        vsl = self.sbuf("sc_vsl", [P, 16, P], BF16)
        ysl2 = [self.sbuf("sc_ysl", [P, P], BF16),
                self.sbuf("sc_ysl2", [P, P], BF16)]
        ysl_ev = [None, None]
        ekh_pl = self.sbuf("sc_ekh", [P, 2048], BF16)
        ek_pl = self.sbuf("sc_ek", [P, 2048], BF16)
        khat_pl = self.sbuf("sc_khp", [P, 2048], BF16)
        rt_pl = self.sbuf("sc_rtp", [P, 2048], BF16)
        rh_pl = self.sbuf("sc_rhp", [P, 2048], BF16)
        arg = self.sbuf("sc_arg", [P, 2048], F32)
        rtf = self.sbuf("sc_rtf", [P, P], BF16)
        rtz = [self.sbuf("sc_rtz0", [P, P], BF16), self.sbuf("sc_rtz1", [P, P], BF16)]
        ktz = [self.sbuf("sc_ktz0", [P, P], BF16), self.sbuf("sc_ktz1", [P, P], BF16)]
        ktok = self.sbuf("sc_ktok", [P, P], BF16)
        am = [self.sbuf("sc_am0", [P, P], BF16), self.sbuf("sc_am1", [P, P], BF16)]
        dg = self.sbuf("sc_dg", [P, P], BF16)
        rku = self.sbuf("sc_rku", [P, P], BF16)
        dd = self.sbuf("sc_dd", [P, 2], F32)
        negone = self.sbuf("sc_no", [P, 2048], BF16)
        Scur = self.sbuf("sc_S", [P, 64], F32)
        Scur_bf = self.sbuf("sc_Sbf", [P, 64], BF16)
        B = self.banks()
        psW = B[2][:, :]
        psA = B[3][:, 0:P]
        psT = B[7][:, 0:P]
        psS = B[4][:, 0:64]
        psY = B[5][:, 0:P]
        psD = B[6][:, 0:2]
        ident = self.sb["ident_bf"]
        ones2 = self.sb["ones2_bf"]
        maskl = self.sb["scanmask_sb"]
        u_sb = self.sb["u_sb"]
        k_sc = self.dram_tmp("k_sc", [CCH, P, T], BF16)
        r_sc = self.dram_tmp("r_sc", [CCH, P, T], BF16)
        v_sc = self.dram_tmp("v_sc", [T // P, P, C], BF16)
        y_sc = self.dram_tmp("y_sc", [T // P, P, C], BF16)
        if save_corr:
            rh_sc = self.dram_tmp("rh_sc", [CCH, P, T], BF16)
            bst_dram = self.dram_tmp("bst_sc", [P, CCH * 16], F32)
        rh_store = None
        bst_store = None

        b.mark("pool", nc.gpsimd.memset(cums[:, 0:1], 0.0))
        b.mark("pool", nc.gpsimd.memset(negone[:], -1.0))
        for z in rtz + ktz:
            b.mark("pool", nc.gpsimd.memset(z[:], 0.0))
        b.wait("dve", b.ms("pool"))

        NA, ND = 16, 32
        for hp in range(min(CCH, SC_HP)):
            b.wait("sp", b.ms("dve"))
            b.wait("dve", b.ms("pe"))
            b.wait("act", b.ms("pe"))
            if rh_store is not None:
                b.wait("dve", rh_store)
            lk = b.mark("sp", nc.sync.dma_start(out=ksl[:, 0:T_], in_=k_sc[hp, :, 0:T_]), 16)
            lv = b.mark("sp", nc.sync.dma_start(
                out=vsl.ap()[:, 0:T_ // P, :], in_=v_sc[0:T_ // P, :, hp * P:(hp + 1) * P]
                .rearrange("g p n -> p g n")), 16)
            if with_y:
                lr = b.mark("sp", nc.sync.dma_start(out=rsl[:, 0:T_], in_=r_sc[hp, :, 0:T_]), 16)
            b.wait("pe", b.ms("dve"))
            b.wait("act", b.ms("dve"))
            for tt in range(T_ // 512):
                mm = nc.tensor.matmul(psW, td2[:, hp * P:(hp + 1) * P],
                                      m2a[:, tt * 512:(tt + 1) * 512],
                                      start=True, stop=True)
                mms = b.mark("pe", mm)
                b.wait("act", mms)
                ev = b.mark("act", nc.scalar.activation(
                    et[:, tt * 512:(tt + 1) * 512], psW, AF.Exp))
                b.wait("pe", ev)
            b.wait("dve", b.ms("act"))
            sc = b.fence("dve", nc.vector.tensor_tensor_scan(
                cums[:, 1:T_ + 1], et[:, 0:T_],
                negone[:, 0:T_], 0.0,
                ALU.subtract, ALU.mult))
            if bst_store is not None:
                b.wait("dve", bst_store)
            b.fence("dve", nc.vector.tensor_copy(
                bst[:, 0:NCH + 1], cums.ap()[:, 0:T_ + 1:LC]))
            if save_corr:
                b.wait("sp", b.ms("dve"))
                bst_store = b.mark("sp", nc.sync.dma_start(
                    out=bst_dram[:, hp * 16:hp * 16 + NCH],
                    in_=bst[:, 0:NCH]), 16)
            b.fence("dve", nc.vector.tensor_add(
                stat[:, 0:NCH], bst[:, 0:NCH], bst[:, 1:NCH + 1]))
            b.fence("dve", nc.vector.tensor_scalar_mul(
                stat[:, 0:NCH], stat[:, 0:NCH], 0.5))
            b.fence("dve", nc.vector.tensor_scalar_mul(
                stat[:, NA:NA + NCH], stat[:, 0:NCH], -1.0))
            mstat = b.fence("dve", nc.vector.tensor_sub(
                stat[:, ND:ND + NCH], bst[:, 1:NCH + 1], bst[:, 0:NCH]))
            b.wait("act", mstat)
            mdt = b.mark("act", nc.scalar.activation(
                stat[:, ND:ND + NCH], stat[:, ND:ND + NCH], AF.Exp))
            b.wait("dve", mdt, lk, lv)
            if with_y:
                b.wait("dve", lr)
            b.fence("dve", nc.vector.tensor_copy(Scur[:, :], S_init.ap()[:, hp, :]))
            b.fence("dve", nc.vector.tensor_copy(Scur_bf[:, :], Scur[:, :]))
            c3 = cums.ap()[:, 1:T_ + 1].rearrange("p (c n) -> p c n", n=LC)
            bb1 = bst.ap()[:, 1:NCH + 1].rearrange("p (c o) -> p c o", o=1)
            mpl = b.fence("dve", nc.vector.tensor_sub(
                arg.ap()[:, 0:T_].rearrange("p (c n) -> p c n", n=LC),
                bb1.broadcast_to([P, NCH, LC]), c3))
            b.wait("act", mpl)
            mpl = b.mark("act", nc.scalar.activation(
                ekh_pl[:, 0:T_], arg[:, 0:T_], AF.Exp))
            b.wait("dve", mpl)
            mkh = b.mark("dve", nc.vector.tensor_mul(
                khat_pl[:, 0:T_], ksl[:, 0:T_], ekh_pl[:, 0:T_]))
            if with_y:
                nc.vector.tensor_scalar(
                    ekh_pl[:, 0:T_], rsl[:, 0:T_], u_sb[:, hp:hp + 1],
                    None, ALU.mult)
                b.mark("dve", nc.vector.tensor_mul(
                    ekh_pl[:, 0:T_], ekh_pl[:, 0:T_], ksl[:, 0:T_]))
            aal = stat.ap()[:, 0:NCH].rearrange("p (c o) -> p c o", o=1)
            mpl = b.fence("dve", nc.vector.tensor_sub(
                arg.ap()[:, 0:T_].rearrange("p (c n) -> p c n", n=LC),
                aal.broadcast_to([P, NCH, LC]), c3))
            b.wait("act", mpl)
            mek = b.mark("act", nc.scalar.activation(
                ek_pl[:, 0:T_], arg[:, 0:T_], AF.Exp))
            if with_y:
                c0v = cums.ap()[:, 0:T_].rearrange("p (c n) -> p c n", n=LC)
                mpl = b.fence("dve", nc.vector.tensor_sub(
                    arg.ap()[:, 0:T_].rearrange("p (c n) -> p c n", n=LC),
                    c0v, aal.broadcast_to([P, NCH, LC])))
                b.wait("act", mpl)
                mer = b.mark("act", nc.scalar.activation(
                    arg[:, 0:T_], arg[:, 0:T_], AF.Exp))
                b.wait("dve", mer)
                mrt = b.fence("dve", nc.vector.tensor_mul(
                    rt_pl[:, 0:T_], rsl[:, 0:T_], arg[:, 0:T_]))
                bb0 = bst.ap()[:, 0:NCH].rearrange("p (c o) -> p c o", o=1)
                mpl = b.fence("dve", nc.vector.tensor_sub(
                    arg.ap()[:, 0:T_].rearrange("p (c n) -> p c n", n=LC),
                    c0v, bb0.broadcast_to([P, NCH, LC])))
                b.wait("act", mpl)
                mer2 = b.mark("act", nc.scalar.activation(
                    arg[:, 0:T_], arg[:, 0:T_], AF.Exp))
                b.wait("dve", mer2)
                mrt = b.mark("dve", nc.vector.tensor_mul(
                    rh_pl[:, 0:T_], rsl[:, 0:T_], arg[:, 0:T_]))
                if save_corr:
                    b.wait("sp", mrt)
                    rh_store = b.mark("sp", nc.sync.dma_start(
                        out=rh_sc[hp, :, 0:T_], in_=rh_pl[:, 0:T_]), 16)
            b.wait("dve", mek)
            if hp == 0 and tag + "_pl" in self.dbg_names:
                self.dbg_dump(tag + "_pl_vsl",
                              vsl.ap().rearrange("p g n -> p (g n)"),
                              [P, 16 * P], dep=b.ms("dve"))
                self.dbg_dump(tag + "_pl_ksl", ksl[:, :], [P, 2048], dep=b.ms("dve"))
                self.dbg_dump(tag + "_pl_cums", cums[:, :], [P, 2049], dep=b.ms("dve"))
                self.dbg_dump(tag + "_pl_bst", bst[:, :], [P, 17], dep=b.ms("dve"))
                self.dbg_dump(tag + "_pl_stat", stat[:, :], [P, 48], dep=b.ms("dve"))
                self.dbg_dump(tag + "_pl_ekh", ekh_pl[:, :], [P, 2048], dep=b.ms("dve"))
                self.dbg_dump(tag + "_pl_ek", ek_pl[:, :], [P, 2048], dep=b.ms("dve"))
                self.dbg_dump(tag + "_pl_khat", khat_pl[:, :], [P, 2048], dep=b.ms("dve"))
            prev_cp = None
            for cc in range(min(NCH, SC_CP)):
                t0 = cc * LC
                b.wait("dve", b.ms("pe"))
                if with_y:
                    nc.vector.tensor_copy(rtz[0][0:64, :], rh_pl[0:64, t0:t0 + P])
                    b.mark("dve", nc.vector.tensor_copy(
                        rtz[1][64:128, :], rh_pl[64:128, t0:t0 + P]))
                nc.vector.tensor_mul(ktz[0][0:64, :], ksl[0:64, t0:t0 + P],
                                     ek_pl[0:64, t0:t0 + P])
                mk = b.mark("dve", nc.vector.tensor_mul(
                    ktz[1][64:128, :], ksl[64:128, t0:t0 + P],
                    ek_pl[64:128, t0:t0 + P]))
                b.wait("pe", mk)
                mt = b.mark("pe", nc.tensor.transpose(psT, khat_pl[:, t0:t0 + P], ident[:]))
                b.wait("dve", mt)
                mtv = b.mark("dve", nc.vector.tensor_copy(ktok[:, :], psT))
                if with_y:
                    md = b.mark("pe", nc.tensor.matmul(
                        psD, ekh_pl[:, t0:t0 + P], ones2[:, :],
                        start=True, stop=True))
                    b.wait("dve", md)
                    mdd = b.fence("dve", nc.vector.tensor_copy(dd[:, :], psD))
                    for hs in range(2):
                        ma = b.mark("pe", nc.tensor.matmul(
                            psA, ktz[hs][:, :], rt_pl[:, t0:t0 + P],
                            start=True, stop=True))
                        mdg = b.mark("dve", nc.vector.tensor_scalar(
                            dg[:, :], ident[:, :], dd[:, hs:hs + 1], None, ALU.mult))
                        b.wait("dve", ma)
                        msel = b.mark("dve", nc.vector.select(
                            am[hs][:, :], maskl.ap(), psA, dg[:, :]))
                        b.wait("pe", msel)
                b.wait("pe", mtv)
                mpe = None
                for hs in range(2):
                    mpe = nc.tensor.matmul(
                        psS[hs * 64:hs * 64 + 64, :],
                        ktok[:, hs * 64:hs * 64 + 64],
                        vsl.ap()[:, cc, hs * 64:hs * 64 + 64],
                        start=True, stop=True, tile_position=(0, hs * 64))
                if with_y:
                    if prev_cp is not None:
                        b.wait("pe", prev_cp)
                    for hs in range(2):
                        nc.tensor.matmul(
                            psY[:, hs * 64:hs * 64 + 64], am[hs][:, :],
                            vsl.ap()[:, cc, hs * 64:hs * 64 + 64],
                            start=True, stop=False)
                        mpe = nc.tensor.matmul(
                            psY[:, hs * 64:hs * 64 + 64], rtz[hs][:, :],
                            Scur_bf[:, :], start=False, stop=True)
                mpe = b.mark("pe", mpe)
                b.wait("dve", mpe)
                mup = b.mark("dve", nc.vector.scalar_tensor_tensor(
                    Scur[:, :], Scur[:, :], stat[:, ND + cc:ND + cc + 1],
                    psS, ALU.mult, ALU.add))
                mupb = b.mark("dve", nc.vector.tensor_copy(Scur_bf[:, :], Scur[:, :]))
                b.wait("pe", mupb)
                if with_y:
                    yp = cc % 2
                    ysl = ysl2[yp]
                    if ysl_ev[yp] is not None:
                        b.wait("dve", ysl_ev[yp])
                    me1 = b.mark("dve", nc.vector.tensor_copy(ysl[:, :], psY))
                    b.wait("sp", me1)
                    ysl_ev[yp] = b.mark("sp", nc.sync.dma_start(
                        out=y_sc[cc, :, hp * P:(hp + 1) * P], in_=ysl[:, :]), 16)
                    prev_cp = me1
                else:
                    prev_cp = b.ms("dve")
            b.fence("dve", nc.vector.tensor_copy(S_init.ap()[:, hp, :], Scur[:, :]))
        b.barrier()
        return b.ms("dve")

    def emit_rwkv_out(self, corr=False):
        nc, b = self.nc, self.b
        if corr:
            rh_sc = self.dram_tmp("rh_sc", [CCH, P, T], BF16)
            rh0 = self.sb["sc_rtp"].ap()[:, 0:CCH * P].rearrange(
                "p (c n) -> p c n", n=P)
            rh1 = self.sb["sc_rhp"].ap()[:, 0:CCH * P].rearrange(
                "p (c n) -> p c n", n=P)
            Sdec = self.sb["sc_rtp"][:, CCH * P:CCH * P + 512]
            S_ex = self.sb["sc_arg"].ap()[:, 512:1024].rearrange(
                "p (c n) -> p c n", c=CCH)
            mz0 = self.b.mark("pool", nc.gpsimd.memset(
                self.sb["sc_rtp"][64:128, 0:CCH * P], 0.0))
            mz1 = self.b.mark("pool", nc.gpsimd.memset(
                self.sb["sc_rhp"][0:64, 0:CCH * P], 0.0))
            self.b.wait("sp", mz0, mz1)
        glnw = self.sb["sc_e"][:, 0:C]
        glnb = self.sb["sc_e"][:, C:2 * C]
        l1 = b.mark("pool", nc.gpsimd.dma_start(out=glnw, in_=self.dram_in("glnw", [P, C])), 16)
        l2 = b.mark("pool", nc.gpsimd.dma_start(out=glnb, in_=self.dram_in("glnb", [P, C])), 16)
        self.b.wait("pool", self.b.ms("pe"))
        wo, wold = self.load_w_bf("wbufA", self.dram_in("Wo", [C, C]), C, C)
        y_sc = self.dram_tmp("y_sc", [T // P, P, C], BF16)
        g_sc = self.dram_tmp("g_sc", [T // P, P, C], BF16)
        s_sc = self.dram_tmp("s_sc", [T, C], F32)
        yt = self.sbuf("scrB1", [P, C], BF16)
        gt = self.sb["sc_ek"][:, 0:C]
        yc = self.sbuf("scrF1", [P, C], F32)
        yn = self.sbuf("scrF2", [P, C], F32)
        yng = self.sb["sc_ek"][:, C:2 * C]
        yngf = self.sb["sc_khp"].ap().rearrange("p (c n) -> p c n", c=CCH)[:, :, 0:P]
        sf = self.sb["sc_cums"][:, 0:C]
        st = self.sb["sc_arg"][:, 0:NHEAD * 4]
        B = self.banks()
        pstr = B[7]
        pso = B[0][:, :]
        pso2 = B[1][:, :]
        S0, S1, S2, S3 = 0, NHEAD, 2 * NHEAD, 3 * NHEAD
        b.wait("dve", l1, l2)
        for g in range(NG):
            yld = b.mark("sp", nc.sync.dma_start(out=yt[:], in_=y_sc[g, :, :]), 16)
            gld = b.mark("sp", nc.sync.dma_start(out=gt, in_=g_sc[g, :, :]), 16)
            if corr:
                rhld = b.mark("sp", nc.sync.dma_start(
                    out=rh0[0:64, :, :],
                    in_=rh_sc[0:CCH, 0:64, g * P:(g + 1) * P]
                    .rearrange("c p n -> p c n")), 16)
                rhld = b.mark("sp", nc.sync.dma_start(
                    out=rh1[64:128, :, :],
                    in_=rh_sc[0:CCH, 64:128, g * P:(g + 1) * P]
                    .rearrange("c p n -> p c n")), 16)
            b.wait("dve", yld)
            if corr:
                psCr = [B[2][:, 0:P], B[3][:, 0:P]]
                psC_ev = [None, None]
                b.wait("pe", rhld)
                for hp in range(CCH):
                    m_ = nc.vector.tensor_scalar(
                        Sdec[:, hp * 64:(hp + 1) * 64], S_ex[:, hp, :],
                        self.sb["sc_arg"][:, 256 + hp * 16 + g:
                                          257 + hp * 16 + g], None, ALU.mult)
                mdq = b.fence("dve", m_)
                b.wait("pe", mdq)
                for hp in range(CCH):
                    ci = hp % 2
                    psC = psCr[ci]
                    if psC_ev[ci] is not None:
                        b.wait("pe", psC_ev[ci])
                    for hs, rhb in ((0, rh0), (1, rh1)):
                        mm = nc.tensor.matmul(
                            psC[:, hs * 64:(hs + 1) * 64],
                            rhb[:, hp, :], Sdec[:, hp * 64:(hp + 1) * 64],
                            start=True, stop=True)
                    mm = b.mark("pe", mm)
                    b.wait("dve", mm)
                    madd = b.mark("dve", nc.vector.tensor_add(
                        yt[:, hp * P:(hp + 1) * P],
                        yt[:, hp * P:(hp + 1) * P], psC))
                    psC_ev[ci] = madd
            y3 = yt.ap().rearrange("p (h n) -> p h n", h=NHEAD)
            b.fence("dve", nc.vector.tensor_reduce(
                st[:, S0:S0 + NHEAD], y3, AX, ALU.add))
            b.fence("dve", nc.vector.tensor_scalar_mul(
                st[:, S1:S1 + NHEAD], st[:, S0:S0 + NHEAD], 1.0 / HK))
            b.fence("dve", nc.vector.tensor_sub(
                yc.ap().rearrange("p (h n) -> p h n", h=NHEAD), y3,
                st[:, S1:S1 + NHEAD].rearrange("p (h o) -> p h o", o=1
                                                    ).broadcast_to([P, NHEAD, HK])))
            nc.vector.tensor_mul(yn[:], yc[:], yc[:])
            b.fence("dve", nc.vector.tensor_reduce(
                st[:, S2:S2 + NHEAD],
                yn.ap().rearrange("p (h n) -> p h n", h=NHEAD), AX, ALU.add))
            m = b.fence("dve", nc.vector.tensor_scalar_mul(
                st[:, S2:S2 + NHEAD], st[:, S2:S2 + NHEAD], 1.0 / HK))
            b.wait("act", m)
            nc.scalar.activation(st[:, S3:S3 + NHEAD], st[:, S2:S2 + NHEAD],
                                 AF.Ln, bias=self.constv(1e-5))
            m = b.mark("act", nc.scalar.activation(
                st[:, S3:S3 + NHEAD], st[:, S3:S3 + NHEAD], AF.Exp, scale=-0.5))
            b.wait("dve", m, gld)
            nc.vector.tensor_mul(
                yn.ap().rearrange("p (h n) -> p h n", h=NHEAD),
                yc.ap().rearrange("p (h n) -> p h n", h=NHEAD),
                st[:, S3:S3 + NHEAD].rearrange("p (h o) -> p h o", o=1
                                                    ).broadcast_to([P, NHEAD, HK]))
            nc.vector.tensor_mul(yn[:], yn[:], glnw)
            nc.vector.tensor_add(yn[:], yn[:], glnb)
            m = b.mark("dve", nc.vector.tensor_mul(yng, yn[:], gt))
            b.wait("pe", m, wold)
            mt = None
            for cc in range(CCH):
                mt = nc.tensor.transpose(pstr[:, cc * P:(cc + 1) * P],
                                         yng[:, cc * P:(cc + 1) * P],
                                         self.sb["ident_bf"][:])
            mt = b.mark("pe", mt)
            b.wait("dve", mt)
            ev = b.mark("dve", nc.vector.tensor_copy(
                yngf, pstr.ap().rearrange("p (c n) -> p c n", c=CCH)))
            b.wait("pe", ev)
            b.wait("sp", ev)
            for half in range(2):
                ps = pso if half == 0 else pso2
                for kc in range(CCH):
                    mm = nc.tensor.matmul(
                        ps, yngf[:, kc, :],
                        wo[:, kc, half * 512:(half + 1) * 512],
                        start=(kc == 0), stop=(kc == CCH - 1))
                mms = b.mark("pe", mm)
                b.wait("dve", mms)
                if g + half >= 1:
                    b.wait("dve", b.ms("sp"))
                ev = b.mark("dve", nc.vector.tensor_copy(
                    sf[:, half * 512:(half + 1) * 512], ps))
                b.wait("pe", ev)
            msf = b.ms("dve")
            b.wait("sp", msf)
            b.mark("sp", nc.sync.dma_start(out=s_sc[g * P:(g + 1) * P, :], in_=sf), 16)
        b.barrier()
        return b.ms("sp")


    def emit_qkv(self, l, xsrc, col0, g0, ngrp, qa, ka, va, kacol0, vgrp0, rot,
                 rotoff, with_q, tag, qcol0=0):
        nc, b = self.nc, self.b
        wq = self.sbuf("f_wqkv", [P, CCH, 3 * INNER], BF16)
        if qa is None:
            qa = self.sbuf("f_qa", [65, FH, 512], BF16)
        qs = self.sbuf("f_qs", [P, FD], F32)
        ks = self.sbuf("f_ks", [P, FD], F32)
        if getattr(self, "_qkv_l", None) != l:
            self._qkv_l = l
            wld = b.mark("pool", nc.gpsimd.dma_start(
                out=wq.ap(), in_=self.dram_in(f"wqkv{l}", [C, 3 * INNER])
                .rearrange("(k p) n -> p k n", p=P)), 16)
            b.mark("pool", nc.gpsimd.dma_start(
                out=qs.ap(), in_=self.dram_in(f"qs{l}", [P, FD])), 16)
            qld = b.mark("pool", nc.gpsimd.dma_start(
                out=ks.ap(), in_=self.dram_in(f"ks{l}", [P, FD])), 16)
            self._qkv_ld = (wld, qld)
        wld, qld = self._qkv_ld
        scr = self.sbuf("f_scr", [P, 512], F32)
        nrm = self.sbuf("f_nrm", [P, 24], F32)
        qrt = self.sbuf("f_qrt", [P, 512], BF16)
        qsw = self.sbuf("f_qsw", [P, 512], BF16)
        tmp = self.sbuf("f_rh", [P, 512], BF16)
        B = self.banks()
        ps = B[0][:, :]
        ps2 = B[1][:, :]
        psT = B[7][:, 0:P]
        ident = self.sb["ident_bf"]
        b.wait("pe", wld, qld)

        def project(gi, which, psb):
            c0 = col0 + gi * P
            for kc in range(CCH):
                xs = xsrc.ap()[:, kc, c0:c0 + P] if kc < 4 else \
                    xsrc.ap()[:, kc, c0 - 1:c0 - 1 + P]
                mm = nc.tensor.matmul(psb, xs, wq[:, kc, which * 512:(which + 1) * 512],
                                      start=(kc == 0), stop=(kc == CCH - 1))
            return b.mark("pe", mm)

        def lnrot(g, which, psb, dst_sbuf):
            cq, sq_ = (rot[0], rot[1]) if which == 0 else (rot[2], rot[3])
            sc_ = qs if which == 0 else ks
            b.wait("act", b.ms("pe"))
            msq = b.mark("act", nc.scalar.activation(scr[:], psb, AF.Square))
            b.wait("dve", msq)
            m = b.fence("dve", nc.vector.tensor_reduce(
                nrm[:, 0:FH], scr.ap().rearrange("p (h n) -> p h n", h=FH),
                AX, ALU.add))
            b.wait("act", m)
            nc.scalar.activation(nrm[:, FH:2 * FH], nrm[:, 0:FH], AF.Ln,
                                 bias=self.constv(1e-30))
            m = b.mark("act", nc.scalar.activation(
                nrm[:, FH:2 * FH], nrm[:, FH:2 * FH], AF.Exp, scale=0.5))
            b.wait("dve", m)
            b.fence("dve", nc.vector.tensor_scalar_max(
                nrm[:, FH:2 * FH], nrm[:, FH:2 * FH], 1e-12))
            b.fence("dve", nc.vector.reciprocal(
                nrm[:, 2 * FH:3 * FH], nrm[:, FH:2 * FH]))
            nc.vector.tensor_mul(
                scr.ap().rearrange("p (h n) -> p h n", h=FH),
                psb.rearrange("p (h n) -> p h n", h=FH),
                nrm.ap()[:, 2 * FH:3 * FH].rearrange("p (h o) -> p h o", o=1)
                .broadcast_to([P, FH, FD]))
            b.mark("dve", nc.vector.tensor_mul(
                scr.ap().rearrange("p (h n) -> p h n", h=FH),
                scr.ap().rearrange("p (h n) -> p h n", h=FH),
                sc_.ap().rearrange("p (o n) -> p o n", o=1).broadcast_to([P, FH, FD])))
            rg = rotoff + g
            s3 = scr.ap().rearrange("p (h a n) -> p h a n", h=FH, a=2)
            c4 = cq.ap()[:, rg, :].rearrange("p (o a n) -> p o a n", o=1, a=2)
            s4 = sq_.ap()[:, rg, :].rearrange("p (o a n) -> p o a n", o=1, a=2)
            d3 = dst_sbuf.ap().rearrange("p (h a n) -> p h a n", h=FH, a=2)
            if self._ho_rel is not None:
                b.wait("dve", self._ho_rel)
            nc.vector.tensor_mul(d3, s3, c4.broadcast_to([P, FH, 2, 32]))
            t3 = tmp.ap().rearrange("p (h a n) -> p h a n", h=FH, a=2)
            nc.vector.tensor_mul(
                t3[:, :, 0, :], s3[:, :, 1, :],
                s4[:, :, 0, :].broadcast_to([P, FH, 32]))
            b.mark("dve", nc.vector.tensor_mul(
                t3[:, :, 1, :], s3[:, :, 0, :],
                s4[:, :, 1, :].broadcast_to([P, FH, 32])))
            return b.mark("dve", nc.vector.tensor_add(
                dst_sbuf[:], dst_sbuf[:], tmp[:]))

        pstr = B[7]

        def heads_out(dst, dstcol):
            q4 = qrt.ap().rearrange("p (h a) -> p h a", a=P)
            w4 = qsw.ap().rearrange("p (h a) -> p h a", a=P)
            if self._ho_rel is not None:
                b.wait("dve", self._ho_rel)
            nc.vector.tensor_copy(w4[:, :, 0:64], q4[:, :, 64:128])
            msw = b.fence("dve", nc.vector.tensor_copy(w4[:, :, 64:128], q4[:, :, 0:64]))
            b.wait("pe", msw)
            for hp in range(4):
                for par in range(2):
                    srcap = (qrt if par == 0 else qsw)[:, hp * P:(hp + 1) * P]
                    mt = b.mark("pe", nc.tensor.transpose(psT, srcap, ident[:]))
                    b.wait("dve", mt)
                    mtv = b.mark("dve", nc.vector.tensor_copy(
                        dst.ap()[0:64, 2 * hp + par, dstcol:dstcol + P],
                        psT[0:64, :]))
                    b.wait("pe", mtv)
            self._ho_rel = b.ms("pe")

        self._t7 = [None] * 4
        self._ho_rel = None
        vb_ = [ps, B[2][:, :]]
        kb_ = [ps2, B[3][:, :]]
        qb_ = [B[4][:, :], B[5][:, :]]
        rel = {}
        for gi in range(ngrp):
            g = g0 + gi
            pa = gi % 2
            if rel.get(("v", pa)) is not None:
                b.wait("pe", rel[("v", pa)])
            mv = project(gi, 2, vb_[pa])
            b.wait("dve", mv)
            mev = b.mark("dve", nc.vector.tensor_copy(
                va.ap()[:, (vgrp0 + gi), :], vb_[pa]))
            rel[("v", pa)] = mev
            if rel.get(("k", pa)) is not None:
                b.wait("pe", rel[("k", pa)])
            project(gi, 1, kb_[pa])
            rel[("k", pa)] = lnrot(g, 1, kb_[pa], qrt)
            heads_out(ka, kacol0 + gi * P)
            if with_q:
                if rel.get(("q", pa)) is not None:
                    b.wait("pe", rel[("q", pa)])
                project(gi, 0, qb_[pa])
                rel[("q", pa)] = lnrot(g, 0, qb_[pa], qrt)
                heads_out(qa, qcol0 + gi * P)


    def emit_attention(self, qa, ka, va, w, mask_prev, o_fm, tag,
                       slots=None, qbase=0):
        nc, b = self.nc, self.b
        Pt = self.sbuf("f_pt", [P, 8, 512], BF16)
        nm = self.sbuf("f_nm", [P, 4], BF16)
        rp0 = self.sbuf("f_rp0", [1, 512], BF16)
        rp1 = self.sbuf("f_rp1", [1, 512], BF16)
        rp32 = self.sbuf("f_rp32", [1, 512], F32)
        rcol = self.sbuf("f_rcol", [P, 2], F32)
        osc = self.sbuf("f_osc", [P, 512], BF16)
        otk = self.sbuf("f_otk", [P, P], BF16)
        onescol = self.sbuf("f_ones1", [P, 1], BF16)
        ones11 = self.sbuf("f_ones11", [1, 1], BF16)
        dmaskT = self.sbuf("f_dmaskT", [P, P], BF16)
        mask0 = self.sbuf("f_mask0", [P, 1], F32)
        pmsb = self.sbuf("f_prevmask", [P, 896], BF16)
        if not getattr(self, "_attn_consts", False):
            self._attn_consts = True
            b.mark("pool", nc.gpsimd.memset(onescol[:], 1.0))
            b.mark("pool", nc.gpsimd.memset(ones11[:], 1.0))
            b.mark("pool", nc.gpsimd.dma_start(
                out=dmaskT.ap(),
                in_=self.dram_in("diag_maskT", [P, P], BF16)), 16)
            b.mark("pool", nc.gpsimd.dma_start(
                out=pmsb.ap(), in_=self.dram_in("prevmask", [P, 896], BF16)), 16)
            mc = b.mark("pool", nc.gpsimd.dma_start(
                out=mask0.ap(), in_=self.dram_in("mask0c", [P, 1])), 16)
            b.wait("pe", mc)
            b.wait("dve", mc)
            b.wait("act", mc)
        B = self.banks()
        psc = [B[0][:, :], B[1][:, :]]
        psQp = B[2][:, :]
        psQo = B[3][:, :]
        psO = B[4][:, :]
        psSum = B[5][0:1, 0:512]
        psRc = B[6][:, 0:2]
        psRow = B[6][0:1, 128:256]
        psT = B[7][:, 0:P]
        ident = self.sb["ident_bf"]

        if slots is None:
            kprev = (w % 3) * 512
            kown = ((w + 1) % 3) * 512
            vb_prev = (((w - 1) * 4) % 8, ((w - 1) * 4 + 1) % 8,
                       ((w - 1) * 4 + 2) % 8, ((w - 1) * 4 + 3) % 8)
            vb_own = ((w * 4) % 8, (w * 4 + 1) % 8,
                      (w * 4 + 2) % 8, (w * 4 + 3) % 8)
        else:
            kprev = slots[0] * 512
            kown = slots[1] * 512
            vb_prev = tuple(slots[2] + i for i in range(4))
            vb_own = tuple(slots[3] + i for i in range(4))
        vgrp = list(vb_prev) + list(vb_own)
        b.wait("pool", b.ms("act"))
        for bb_ in range(1, 4):
            b.mark("pool", nc.gpsimd.memset(Pt[:, 4 + bb_, 0:bb_ * P], 0.0))
        mz = b.ms("pool")
        b.wait("act", mz)
        psc4 = [B[0][:, :], B[1][:, :], B[2][:, :], B[3][:, :]]
        psc_ev = [None] * 4
        rrc = [0]
        b6_ev = [None]

        def next_bank():
            i = rrc[0] % 4
            rrc[0] += 1
            if psc_ev[i] is not None:
                b.wait("pe", psc_ev[i])
            return i

        for h in range(FH):
            mq = None
            for p in range(4):
                qsl = qa.ap()[0:64, h, qbase + p * P:qbase + (p + 1) * P]
                now = (p + 1) * P
                ia = next_bank()
                ib = next_bank()
                nc.tensor.matmul(psc4[ia], qsl,
                                 ka.ap()[0:64, h, kprev:kprev + 512],
                                 start=True, stop=True)
                mp = b.mark("pe", nc.tensor.matmul(
                    psc4[ib][:, 0:now], qsl,
                    ka.ap()[0:64, h, kown:kown + now],
                    start=True, stop=True))
                b.wait("dve", mp)
                psc_ev[ia] = b.fence("dve", nc.vector.reduce_max(
                    nm[:, 0:1], psc4[ia], AX, negate=True))
                psc_ev[ib] = b.fence("dve", nc.vector.reduce_max(
                    nm[:, 1:2], psc4[ib][:, 0:now], AX, negate=True))
                mnm = b.mark("dve", nc.vector.tensor_tensor(
                    nm[:, 2:3], nm[:, 0:1], nm[:, 1:2], ALU.min))
                b.wait("pe", mnm)
                if b6_ev[0] is not None:
                    b.wait("pe", b6_ev[0])
                mt = b.mark("pe", nc.tensor.matmul(
                    psRow, nm[:, 2:3], ident[:], start=True, stop=True))
                b.wait("dve", mt)
                mq = b.mark("dve", nc.vector.tensor_copy(
                    qa.ap()[64:65, h, qbase + p * P:qbase + (p + 1) * P], psRow))
                b6_ev[0] = mq
            b.wait("pe", mq)
            for kb in range(8):
                i = next_bank()
                pb = psc4[i]
                own = kb >= 4
                bb_ = kb - 4
                kcc = (kprev + kb * P) if kb < 4 else (kown + (kb - 4) * P)
                mm = b.mark("pe", nc.tensor.matmul(
                    pb, ka.ap()[0:65, h, kcc:kcc + P],
                    qa.ap()[0:65, h, qbase:qbase + 512],
                    start=True, stop=True))
                cA = bb_ * P if own and bb_ > 0 else 0
                md = None
                if own:
                    b.wait("dve", mm)
                    md = b.mark("dve", nc.vector.tensor_add(
                        pb[:, bb_ * P:(bb_ + 1) * P],
                        pb[:, bb_ * P:(bb_ + 1) * P], dmaskT[:]))
                else:
                    b.wait("dve", mm)
                    c0m = kb * P
                    md = b.mark("dve", nc.vector.tensor_add(
                        pb[:, c0m:512], pb[:, c0m:512],
                        pmsb[:, 384:896 - kb * P]))
                    if mask_prev:
                        md = b.mark("dve", nc.vector.tensor_scalar(
                            pb, pb, mask0[:, 0:1], None, ALU.add))
                b.wait("act", mm)
                if md:
                    b.wait("act", md)
                me = b.mark("act", nc.scalar.activation(
                    Pt[:, kb, cA:512], pb[:, cA:512], AF.Exp))
                psc_ev[i] = me
            b.wait("pe", b.ms("act"))
            for kb in range(8):
                ms = nc.tensor.matmul(psSum, onescol[:], Pt[:, kb, :],
                                      start=(kb == 0), stop=(kb == 7))
            ms = b.mark("pe", ms)
            b.wait("dve", ms)
            b.fence("dve", nc.vector.reciprocal(rp32[0:1, :], psSum))
            mr = b.fence("dve", nc.vector.tensor_copy(
                (rp0 if h % 2 == 0 else rp1)[0:1, :], rp32[0:1, :]))
            b.wait("pe", mr)
            hb = (h % 2) * 64
            for kb in range(8):
                mo = nc.tensor.matmul(
                    psO[hb:hb + 64, :],
                    va.ap()[:, vgrp[kb], h * 64:(h + 1) * 64],
                    Pt[:, kb, :], start=(kb == 0), stop=(kb == 7),
                    tile_position=(0, hb))
            mo = b.mark("pe", mo)
            if h % 2 == 1:
                hp = h // 2
                b.wait("dve", mo)
                mev = b.fence("dve", nc.vector.tensor_copy(osc[:], psO))
                b.wait("pe", mev)
                for p in range(4):
                    nc.tensor.matmul(psRc[:, 0:1], rp0[0:1, p * P:(p + 1) * P],
                                     ones11[:], start=True, stop=True)
                    mrc = b.mark("pe", nc.tensor.matmul(
                        psRc[:, 1:2], rp1[0:1, p * P:(p + 1) * P],
                        ones11[:], start=True, stop=True))
                    b.wait("dve", mrc)
                    mrcv = b.fence("dve", nc.vector.tensor_copy(rcol[:], psRc))
                    b.wait("pe", mrcv)
                    mt = b.mark("pe", nc.tensor.transpose(
                        psT, osc[:, p * P:(p + 1) * P], ident[:]))
                    b.wait("dve", mt)
                    nc.vector.tensor_scalar(otk[:, 0:64], psT[:, 0:64],
                                            rcol[:, 0:1], None, ALU.mult)
                    mts = b.fence("dve", nc.vector.tensor_scalar(
                        otk[:, 64:128], psT[:, 64:128],
                        rcol[:, 1:2], None, ALU.mult))
                    b.wait("pe", mts)
                    mt2 = b.mark("pe", nc.tensor.transpose(psT, otk[:], ident[:]))
                    b.wait("dve", mt2)
                    mo2 = b.mark("dve", nc.vector.tensor_copy(
                        o_fm.ap()[:, hp, p * P:(p + 1) * P], psT))
                    b.wait("pe", mo2)

    def emit_wo_resid(self, l, o_fm, xsrc, xcol, dst, dcol, tag):
        nc, b = self.nc, self.b
        wo = self.sbuf("f_wo", [P, 4, C], BF16)
        key = (l, tag)
        if getattr(self, "_wo_l", None) != key:
            self._wo_l = key
            b.wait("pool", b.ms("pe"))
            self._wo_ld = b.mark("pool", nc.gpsimd.dma_start(
                out=wo.ap(), in_=self.dram_in(f"wo{l}", [INNER, C])
                .rearrange("(k p) n -> p k n", p=P)), 16)
        b.wait("pe", self._wo_ld)
        B = self.banks()
        wo_ev = [None, None, None]
        for oc in range(CCH):
            i = oc % 3
            ps = B[i][:, :]
            if wo_ev[i] is not None:
                b.wait("pe", wo_ev[i])
            for kc in range(4):
                mm = nc.tensor.matmul(ps, wo[:, kc, oc * P:(oc + 1) * P],
                                      o_fm.ap()[:, kc, 0:512],
                                      start=(kc == 0), stop=(kc == 3))
            mm = b.mark("pe", mm)
            b.wait("dve", mm)
            ev = b.mark("dve", nc.vector.tensor_add(
                dst.ap()[:, oc, dcol:dcol + 512], ps,
                xsrc.ap()[:, oc, xcol:xcol + 512]))
            wo_ev[i] = ev
        b.wait("pe", b.ms("dve"))

    def emit_ff(self, l, stream, col0, ntok, tag, stash_dram=None,
                stash_ms=None):
        nc, b = self.nc, self.b
        TL = 512 if ntok % 512 == 0 else 256
        w1b = [self.sbuf("f_w1a", [P, CCH, 512], BF16),
               self.sbuf("f_w1b", [P, CCH, 512], BF16)]
        w2b = [self.sbuf("f_w2a", [P, 22, P], BF16),
               self.sbuf("f_w2b", [P, 22, P], BF16)]
        b1a = self.sbuf("f_b1a", [P, 22], F32)
        b1g = self.sbuf("f_b1g", [P, 22], F32)
        b2c = self.sbuf("f_b2c", [P, CCH], F32)
        onesrow = self.sbuf("f_onesr", [1, P], BF16)
        lnwc = self.sbuf("f_lnw", [P, CCH], F32)
        lnbc = self.sbuf("f_lnb", [P, CCH], F32)
        hs = self.sbuf("f_hs", [P, CCH, 512], BF16)
        hstash = self.sbuf("f_hst", [P, CCH], BF16)
        sq = self.sbuf("f_sq", [P, 512], BF16)
        lno = self.sbuf("f_lno", [P, CCH, 512], BF16)
        ffin = self.sbuf("f_ffin", [P, 22, 512], BF16)
        gg = self.sbuf("f_gg", [P, 512], BF16)
        scr1 = self.sbuf("f_scrf", [P, 512], F32)
        murow = self.sbuf("f_mur", [1, 2 * 512], BF16)
        vrow = self.sbuf("f_vr", [1, 512], F32)
        onescol = self.sbuf("f_ones1c", [P, 1], BF16)
        key = (l, tag)
        if getattr(self, "_ff_l", None) != key:
            self._ff_l = key
            b.wait("pool", b.ms("pe"))
            b.mark("pool", nc.gpsimd.memset(onesrow[:], 1.0))
            b.mark("pool", nc.gpsimd.memset(onescol[:], 1.0))
            b.mark("pool", nc.gpsimd.dma_start(
                out=b1a.ap(), in_=self.dram_in(f"b1a{l}", [P, 22])), 16)
            b.mark("pool", nc.gpsimd.dma_start(
                out=b1g.ap(), in_=self.dram_in(f"b1g{l}", [P, 22])), 16)
            b.mark("pool", nc.gpsimd.dma_start(
                out=b2c.ap(), in_=self.dram_in(f"b2c{l}", [P, CCH])), 16)
            b.mark("pool", nc.gpsimd.dma_start(
                out=lnwc.ap(), in_=self.dram_in(f"flnw{l}", [P, CCH])), 16)
            cld = b.mark("pool", nc.gpsimd.dma_start(
                out=lnbc.ap(), in_=self.dram_in(f"flnb{l}", [P, CCH])), 16)
            self._ff_ld = cld
        b.wait("pe", self._ff_ld)
        b.wait("dve", self._ff_ld)
        w1d = self.dram_in(f"w1{l}", [C, 2 * HIDP])
        w2d = self.dram_in(f"w2{l}", [HIDP, C])
        B = self.banks()
        psA_ = [B[0][:, :], B[1][:, :]]
        psS1 = B[2][0:1, 0:TL]
        psS2 = B[3][0:1, 0:TL]
        psMu = B[4][:, 0:TL]
        psRs = B[5][:, 0:TL]
        psA_ = [B[0][:, 0:TL], B[1][:, 0:TL]]
        ntile = ntok // TL
        psR = [B[0][:, 0:TL], B[1][:, 0:TL], B[6][:, 0:TL]]
        bank_ev = [None, None, None]
        wb_ev = {}
        gg_ev = [None, None]
        ggb = [gg, self.sbuf("f_gg2", [P, 512], BF16)]
        rr = 0
        b.wait("dve", b.ms("pe"))
        if stash_dram is None:
            b.fence("dve", nc.vector.tensor_copy(
                hstash.ap().rearrange("p (c o) -> p c o", o=1),
                stream.ap()[:, :, col0 - 1:col0]))
        else:
            if stash_ms is not None:
                b.wait("sp", stash_ms)
            mld = b.mark("sp", nc.sync.dma_start(
                out=hstash.ap(), in_=stash_dram), 16)
            b.wait("dve", mld)
            b.fence("dve", nc.vector.tensor_scalar(
                hstash.ap(), hstash.ap(), self.sb["hgate"][:], None, ALU.mult))
        for t in range(ntile):
            c0 = col0 + t * TL
            b.wait("dve", b.ms("pe"))
            for cc in range(CCH):
                if cc < 4:
                    nc.vector.tensor_copy(hs[:, cc, 0:TL], stream.ap()[:, cc, c0:c0 + TL])
                else:
                    nc.vector.tensor_copy(hs[:, cc, 0:1], hstash[:, cc:cc + 1])
                    nc.vector.tensor_copy(hs[:, cc, 1:TL],
                                          stream.ap()[:, cc, c0:c0 + TL - 1])
            mh = b.fence("dve", nc.vector.tensor_copy(
                hstash.ap().rearrange("p (c o) -> p c o", o=1),
                stream.ap()[:, :, c0 + TL - 1:c0 + TL]))
            b.wait("act", mh)
            b.wait("pe", mh)
            for cc in range(CCH):
                msq = b.mark("act", nc.scalar.activation(
                    sq[:, 0:TL], hs[:, cc, 0:TL], AF.Square))
                b.wait("pe", msq)
                nc.tensor.matmul(psS1, onescol[:], hs[:, cc, 0:TL],
                                 start=(cc == 0), stop=(cc == CCH - 1))
                mpe = b.mark("pe", nc.tensor.matmul(
                    psS2, onescol[:], sq[:, 0:TL],
                    start=(cc == 0), stop=(cc == CCH - 1)))
                b.wait("act", mpe)
            b.wait("dve", b.ms("pe"))
            b.fence("dve", nc.vector.tensor_scalar_mul(
                murow[0:1, 0:TL], psS1, 1.0 / C))
            b.fence("dve", nc.vector.tensor_scalar_mul(
                vrow[0:1, 0:TL], psS2, 1.0 / C))
            nc.vector.tensor_copy(scr1[0:1, 0:TL], murow[0:1, 0:TL])
            nc.vector.tensor_mul(
                scr1[0:1, 0:TL], scr1[0:1, 0:TL], scr1[0:1, 0:TL])
            m = b.mark("dve", nc.vector.tensor_sub(
                vrow[0:1, 0:TL], vrow[0:1, 0:TL], scr1[0:1, 0:TL]))
            b.wait("act", m)
            nc.scalar.activation(vrow[0:1, 0:TL], vrow[0:1, 0:TL], AF.Ln,
                                 bias=self.constv(1e-5)[0:1, :])
            m = b.mark("act", nc.scalar.activation(
                vrow[0:1, 0:TL], vrow[0:1, 0:TL], AF.Exp, scale=-0.5))
            b.wait("dve", m)
            m = b.fence("dve", nc.vector.tensor_copy(
                murow[0:1, 512:512 + TL], vrow[0:1, 0:TL]))
            b.wait("pe", m)
            nc.tensor.matmul(psMu, onesrow[0:1, :], murow[0:1, 0:TL],
                             start=True, stop=True)
            mbc = b.mark("pe", nc.tensor.matmul(
                psRs, onesrow[0:1, :], murow[0:1, 512:512 + TL],
                start=True, stop=True))
            b.wait("dve", mbc)
            for cc in range(CCH):
                nc.vector.scalar_tensor_tensor(
                    scr1[:, 0:TL], hs[:, cc, 0:TL], 1.0, psMu, ALU.mult,
                    ALU.subtract)
                nc.vector.tensor_mul(scr1[:, 0:TL], scr1[:, 0:TL], psRs)
                m = b.fence("dve", nc.vector.tensor_scalar(
                    lno[:, cc, 0:TL], scr1[:, 0:TL], lnwc[:, cc:cc + 1],
                    lnbc[:, cc:cc + 1], ALU.mult, ALU.add))
            b.wait("pe", m)
            if t == 0 and tag + "_lno" in self.dbg_names:
                self.dbg_dump(tag + "_lno", lno.ap().rearrange("p c n -> p (c n)"),
                              [P, CCH * TL], dep=b.ms("dve"))
            for half in range(2):
                for c5 in range((HIDP + 511) // 512):
                    pw = min(512, HIDP - c5 * 512)
                    bi = c5 % 2
                    buf = w1b[bi]
                    if wb_ev.get(("w1", bi)) is not None:
                        b.wait("pool", wb_ev[("w1", bi)])
                    wld = b.mark("pool", nc.gpsimd.dma_start(
                        out=buf.ap()[:, :, 0:pw],
                        in_=w1d[:, half * HIDP + c5 * 512:half * HIDP + c5 * 512 + pw]
                        .rearrange("(k p) n -> p k n", p=P)), 16)
                    b.wait("pe", wld)
                    for oc in range(pw // P):
                        ochunk = c5 * 4 + oc
                        if ochunk >= 22:
                            continue
                        i = rr % 3
                        rr += 1
                        ps = psR[i]
                        if bank_ev[i] is not None:
                            b.wait("pe", bank_ev[i])
                        for kc in range(CCH):
                            mm = nc.tensor.matmul(
                                ps, buf[:, kc, oc * P:(oc + 1) * P],
                                lno[:, kc, 0:TL],
                                start=(kc == 0), stop=(kc == CCH - 1))
                        mm = b.mark("pe", mm)
                        if half == 0:
                            b.wait("dve", mm)
                            ev = b.mark("dve", nc.vector.tensor_scalar(
                                ffin[:, ochunk, 0:TL], ps[:, 0:TL],
                                b1a[:, ochunk:ochunk + 1], None, ALU.add))
                            bank_ev[i] = ev
                        else:
                            b.wait("dve", mm)
                            ma = b.fence("dve", nc.vector.tensor_scalar(
                                ps[:, 0:TL], ps[:, 0:TL],
                                b1g[:, ochunk:ochunk + 1], None, ALU.add))
                            gi_ = rr % 2
                            gbuf = ggb[gi_]
                            b.wait("act", ma)
                            if gg_ev[gi_] is not None:
                                b.wait("act", gg_ev[gi_])
                            ga = b.mark("act", nc.scalar.activation(
                                gbuf[:, 0:TL], ps[:, 0:TL], AF.Gelu))
                            bank_ev[i] = ga
                            b.wait("pool", ga)
                            b.wait("pool", b.ms("dve"))
                            gp = b.mark("pool", nc.gpsimd.tensor_mul(
                                ffin[:, ochunk, 0:TL], ffin[:, ochunk, 0:TL],
                                gbuf[:, 0:TL]))
                            gg_ev[gi_] = gp
                    wb_ev[("w1", bi)] = b.ms("pe")
            if t == 0 and tag + "_ffin" in self.dbg_names:
                b.barrier(("pool", "dve", "pe", "act"))
                self.dbg_dump(tag + "_ffin", ffin.ap().rearrange("p c n -> p (c n)"),
                              [P, 22 * TL], dep=b.ms("pool"))
            b.wait("pe", b.ms("pool"))
            for oc in range(CCH):
                bi = oc % 2
                buf = w2b[bi]
                if wb_ev.get(("w2", bi)) is not None:
                    b.wait("pool", wb_ev[("w2", bi)])
                wld = b.mark("pool", nc.gpsimd.dma_start(
                    out=buf.ap(), in_=w2d[:, oc * P:(oc + 1) * P]
                    .rearrange("(k p) n -> p k n", p=P)), 16)
                b.wait("pe", wld)
                i = rr % 3
                rr += 1
                ps = psR[i]
                if bank_ev[i] is not None:
                    b.wait("pe", bank_ev[i])
                for kc in range(22):
                    mm = nc.tensor.matmul(ps[:, 0:TL], buf[:, kc, :],
                                          ffin[:, kc, 0:TL],
                                          start=(kc == 0), stop=(kc == 21))
                mm = b.mark("pe", mm)
                wb_ev[("w2", bi)] = (mm[0], mm[1])
                b.wait("dve", mm)
                ev = b.mark("dve", nc.vector.scalar_tensor_tensor(
                    stream.ap()[:, oc, c0:c0 + TL], ps[:, 0:TL],
                    b2c[:, oc:oc + 1],
                    stream.ap()[:, oc, c0:c0 + TL], ALU.add, ALU.add))
                bank_ev[i] = ev
        b.barrier()
        self.sbuf_free_from("f_w1a")
        self._ff_l = None

    def emit_ffn_layer(self, l, xsrc, xcol0, dst, dcol0, rot, tag, xch,
                       dst_ap=None, col0_ms=None):
        nc, b = self.nc, self.b
        if dst_ap is not None:
            class _W:
                def __init__(self, a): self._a = a
                def ap(self): return self._a
            dst = _W(dst_ap)
        hgate = self.sb["hgate"]
        hginv = self.sb["hginv"]
        qa = self.sbuf("f_qa", [65, FH, 2 * 512], BF16)
        ka = self.sbuf("f_ka", [65, FH, 4 * 512], BF16)
        va = self.sbuf("f_va", [P, 16, 512], BF16)
        o_fm = self.sbuf("f_ofm", [P, 4, 512], BF16)
        b.mark("pool", nc.gpsimd.memset(
            ka[0:64, :, :].rearrange("p h n -> p (h n)"), 0.0))
        b.mark("pool", nc.gpsimd.memset(
            va.ap().rearrange("p g n -> p (g n)"), 0.0))
        for h in range(FH):
            b.mark("pool", nc.gpsimd.memset(ka[64:65, h, :], 1.0))
            b.mark("pool", nc.gpsimd.memset(qa[64:65, h, :], 0.0))
        b.barrier(("pool", "pe", "dve", "act"))

        KS = {2: 0, 3: 1, 0: 3, 1: 1}
        KSP = {3: 0, 0: 2, 1: 3, 2: 1}
        VB = {2: 0, 3: 4, 0: 12, 1: 4}
        VBP = {3: 0, 0: 8, 1: 12, 2: 4}
        QB = {2: 0, 3: 512, 0: 512, 1: 512}

        def qkv(w):
            self.emit_qkv(l, xsrc, xcol0 + w * 512, w * 4, 4, qa, ka, va,
                          KS[w] * 512, VB[w], rot, 0, True, tag, qcol0=QB[w])
            b.barrier(("pe", "dve", "act", "pool"))

        def attn_wo(w):
            self.emit_attention(qa, ka, va, w, w == 0, o_fm, tag,
                                slots=(KSP[w], KS[w], VBP[w], VB[w]),
                                qbase=QB[w])
            b.barrier(("pe", "dve", "act", "pool"))
            self.emit_wo_resid(l, o_fm, xsrc, xcol0 + w * 512, dst,
                               dcol0 + w * 512, tag)
            b.barrier(("pe", "dve", "act", "pool"))

        qkv(2)
        qkv(3)
        mg = b.fence("dve", nc.vector.tensor_scalar(
            ka[0:64, :, 1024:1536], ka[0:64, :, 512:1024],
            hginv[0:64, :], None, ALU.mult))
        mg = b.fence("dve", nc.vector.tensor_scalar(
            va[:, 8:12, :], va[:, 4:8, :], hginv[:], None, ALU.mult))
        import os
        XKV = os.environ.get("XCH_KV", "1") == "1"
        XPA = os.environ.get("XCH_PA", "1") == "1"
        b.wait("sp", mg)
        mk = b.mark("sp", nc.sync.dma_start(
            out=xch["kb_i"].rearrange("p (h n) -> p h n", h=FH),
            in_=ka[0:64, :, 1024:1536]), 16)
        mv = b.mark("sp", nc.sync.dma_start(
            out=xch["vb_i"].rearrange("p (g n) -> p g n", g=4),
            in_=va[:, 8:12, :]), 16)
        b.wait("pool", mk, mv)
        if XKV:
            b.mark("cc", nc.gpsimd.collective_compute(
                "AllReduce", mybir.AluOpType.add,
                replica_groups=[[0, 1], [2, 3], [4, 5], [6, 7]],
                ins=[xch["kb_i"].opt()], outs=[xch["kb_o"].opt()]), 1)
            mcc_kv = b.mark("cc", nc.gpsimd.collective_compute(
                "AllReduce", mybir.AluOpType.add,
                replica_groups=[[0, 1], [2, 3], [4, 5], [6, 7]],
                ins=[xch["vb_i"].opt()], outs=[xch["vb_o"].opt()]), 1)
        else:
            mcc_kv = None
        attn_wo(3)
        pa_col = dcol0 + 2047
        pst = self.sbuf("f_past", [P, CCH], BF16)
        mpa = b.fence("dve", nc.vector.tensor_scalar(
            pst.ap().rearrange("p (c o) -> p c o", o=1),
            dst.ap()[:, :, pa_col:pa_col + 1], hginv[:], None, ALU.mult))
        b.wait("sp", mpa)
        mpd = b.mark("sp", nc.sync.dma_start(out=xch["pa_i"], in_=pst.ap()), 16)
        b.wait("pool", mpd)
        if XPA:
            mcc_pa = b.mark("cc", nc.gpsimd.collective_compute(
                "AllReduce", mybir.AluOpType.add,
                replica_groups=[[0, 1], [2, 3], [4, 5], [6, 7]],
                ins=[xch["pa_i"].opt()], outs=[xch["pa_o"].opt()]), 1)
        else:
            mcc_pa = None
        if col0_ms is not None:
            b.wait("pe", col0_ms)
        qkv(0)
        if XKV:
            b.wait("pool", mcc_kv)
            mlk = b.mark("pool", nc.gpsimd.dma_start(
                out=ka[0:64, :, 1024:1536],
                in_=xch["kb_o"].rearrange("p (h n) -> p h n", h=FH)), 16)
            mlv = b.mark("pool", nc.gpsimd.dma_start(
                out=va[:, 8:12, :],
                in_=xch["vb_o"].rearrange("p (g n) -> p g n", g=4)), 16)
            b.wait("dve", mlk, mlv)
            nc.vector.tensor_scalar(
                ka[0:64, :, 1024:1536], ka[0:64, :, 1024:1536],
                hgate[0:64, :], None, ALU.mult)
            mgk = b.fence("dve", nc.vector.tensor_scalar(
                va[:, 8:12, :], va[:, 8:12, :], hgate[:], None, ALU.mult))
            b.wait("pe", mgk)
        attn_wo(0)
        qkv(1)
        attn_wo(1)
        attn_wo(2)
        self.dbg_dump(tag + "_qa2", qa.ap().rearrange("p h n -> p (h n)"),
                      [65, FH * 1024], dep=b.ms("dve"))
        b.barrier(("pool", "sp", "pe", "dve", "act"))
        self.sbuf_free_from("f_qa")
        self._attn_consts = False
        self._wo_l = None
        self._qkv_l = None
        if os.environ.get("FF_SKIP", "0") != "1":
            self.emit_ff(l, dst, dcol0, T, tag,
                         stash_dram=xch["pa_o"] if XPA else None,
                         stash_ms=mcc_pa)


    def emit_final(self, x2v, out_dram):
        nc, b = self.nc, self.b
        g2g = [self.sbuf("fin_g2", [P, CCH, P], BF16),
               self.sbuf("fin_g2b", [P, CCH, P], BF16)]
        g2t = [self.sbuf("fin_g2t", [P, C], BF16),
               self.sbuf("fin_g2tb", [P, C], BF16)]
        xg = [self.sbuf("fin_x", [P, C], F32), self.sbuf("fin_xb", [P, C], F32)]
        sg = [self.sbuf("fin_s", [P, C], F32), self.sbuf("fin_sb", [P, C], F32)]
        og = [self.sbuf("fin_o", [P, C], F32), self.sbuf("fin_ob", [P, C], F32)]
        s_sc = self.dram_tmp("s_sc", [T, C], F32)
        xmd = self.dram_in("xm", [NGH * P, C])
        B = self.banks()
        pstr = B[7]
        ident = self.sb["ident_bf"]
        ld_ev = [None, None]
        sil_ev = [None, None]
        b7_ev = None
        mo_ev = [None, None]
        for g in range(NG):
            par = g % 2
            if mo_ev[par] is not None:
                b.wait("sp", mo_ev[par])
            lx = b.mark("sp", nc.sync.dma_start(
                out=xg[par][:], in_=xmd[(g + 1) * P:(g + 2) * P, :]), 16)
            ls = b.mark("sp", nc.sync.dma_start(
                out=sg[par][:], in_=s_sc[g * P:(g + 1) * P, :]), 16)
            if sil_ev[par] is not None:
                b.wait("act", sil_ev[par])
            elif g == 0:
                b.wait("act", b.ms("pe"))
            ma = b.mark("act", nc.scalar.activation(
                g2g[par].ap(), x2v[:, :, 1 + g * P:1 + (g + 1) * P], AF.Silu))
            b.wait("pe", ma)
            if b7_ev is not None:
                b.wait("pe", b7_ev)
            mt = None
            for cc in range(CCH):
                mt = nc.tensor.transpose(pstr[:, cc * P:(cc + 1) * P],
                                         g2g[par][:, cc, :], ident[:])
            mt = b.mark("pe", mt)
            sil_ev[par] = mt
            b.wait("dve", mt)
            b7_ev = b.mark("dve", nc.vector.tensor_copy(
                g2t[par].ap().rearrange("p (c n) -> p c n", c=CCH),
                pstr.ap().rearrange("p (c n) -> p c n", c=CCH)))
            b.wait("dve", lx, ls)
            if ld_ev[par] is not None:
                b.wait("dve", ld_ev[par])
            nc.vector.tensor_mul(og[par][:], sg[par][:], g2t[par][:])
            mo = b.fence("dve", nc.vector.tensor_add(
                og[par][:], og[par][:], xg[par][:]))
            mo_ev[par] = mo
            b.wait("sp", mo)
            ld_ev[par] = b.mark("sp", nc.sync.dma_start(
                out=out_dram[g * P:(g + 1) * P, :], in_=og[par][:]), 16)
        b.barrier()

def build(dbg=(), stage_limit=99):
    import os
    stage_limit = int(os.environ.get("STAGE_LIMIT", stage_limit))
    k = K(dbg)
    nc, b = k.nc, k.b

    xm = k.dram_in("xm", [NGH * P, C])
    ident = k.dram_in("ident", [P, P])
    ones2 = k.dram_in("ones2", [P, 2])
    scanmask = k.dram_in("scan_mask", [P, P], U8)
    u_fm = k.dram_in("u_fm", [P, CCH])

    k.sbuf("ident_bf", [P, P], BF16)
    k.sbuf("ones2_bf", [P, 2], BF16)
    k.sbuf("scanmask_sb", [P, P], U8)
    k.sbuf("u_sb", [P, CCH], F32)
    hgate = k.sbuf("hgate", [P, 1], F32)
    hginv = k.sbuf("hginv", [P, 1], F32)
    xn_fm = k.sbuf("xn_fm", [P, CCH, NGH * P], BF16)
    S_init = k.sbuf("S_init", [P, CCH, 64], F32)
    out_dram = k.dram_out("out", [T, C], F32)

    for nm, src_ in (("ident_bf", ident), ("ones2_bf", ones2),
                     ("scanmask_sb", scanmask), ("u_sb", u_fm)):
        b.mark("pool", nc.gpsimd.dma_start(out=k.sb[nm].ap(), in_=src_), 16)
    b.mark("pool", nc.gpsimd.dma_start(
        out=hgate.ap(), in_=k.dram_in("halo_gate", [P, 1])), 16)
    b.mark("pool", nc.gpsimd.dma_start(
        out=hginv.ap(), in_=k.dram_in("halo_gate_inv", [P, 1])), 16)
    b.mark("pool", nc.gpsimd.memset(S_init.ap(), 0.0))
    k.constv(1e-5)
    k.constv(1e-30)
    b.barrier(("pool", "pe", "dve", "act", "sp"))

    if stage_limit >= 5:
        k.emit_ln_fm(xm, xn_fm.ap(), NGH)
        b.barrier()
    k.dbg_dump("xn_fm", xn_fm.ap().rearrange("p c n -> p (c n)"),
               [P, CCH * NGH * P], dep=k.b.ms("dve"))
    if stage_limit >= 6:
        m2a, td2 = k.emit_rwkv_mix(xn_fm, "wkvrg", T, "main")
    if stage_limit >= 7:
        k.emit_rwkv_scan("main", T, m2a, td2, S_init, with_y=True,
                         save_corr=True)
    if stage_limit >= 8:
        b.barrier()
        sfin_b = k.dram_tmp("sfin_b", [P, CCH * 64], F32)
        sfin_g = k.dram_tmp("sfin_g", [P, CCH * 64], F32)
        arg_sb = k.sb["sc_arg"]
        S_ex = arg_sb.ap()[:, 512:1024].rearrange("p (c n) -> p c n", c=CCH)
        expb = arg_sb[:, 256:384]
        bst_tmp = arg_sb[:, 384:512]
        sfin_stage = arg_sb.ap()[:, 1024:1536].rearrange(
            "p (c n) -> p c n", c=CCH)
        mgt = b.fence("dve", nc.vector.tensor_scalar(
            sfin_stage, S_init.ap(), hginv[:], None, ALU.mult))
        b.wait("sp", mgt)
        mS = b.mark("sp", nc.sync.dma_start(
            out=sfin_b, in_=arg_sb[:, 1024:1536]), 16)
        b.wait("pool", mS)
        mcc = b.mark("cc", nc.gpsimd.collective_compute(
            "AllReduce", mybir.AluOpType.add,
            replica_groups=[[0, 1], [2, 3], [4, 5], [6, 7]],
            ins=[sfin_b.opt()], outs=[sfin_g.opt()]), 1)
        b.wait("pool", mcc)
        mL = b.mark("pool", nc.gpsimd.dma_start(
            out=S_ex, in_=sfin_g.rearrange("p (c n) -> p c n", c=CCH)), 16)
        mB = b.mark("pool", nc.gpsimd.dma_start(
            out=bst_tmp, in_=k.tmp["bst_sc"]), 16)
        b.wait("dve", mL)
        b.fence("dve", nc.vector.tensor_scalar(
            S_ex, S_ex, hgate[:], None, ALU.mult))
        b.wait("act", b.ms("dve"), mB)
        b.mark("act", nc.scalar.activation(expb, bst_tmp, AF.Exp))
        b.barrier(("act", "dve", "pe", "sp", "pool"))
        k.emit_rwkv_out(corr=True)

    if stage_limit >= 9:
        b.barrier()
        k.sbuf_free_from("wsmall")
        rot = [k.sbuf(f"f_rot{i}", [P, NG, FD], BF16) for i in range(4)]
        for i in range(4):
            b.mark("pool", nc.gpsimd.dma_start(
                out=rot[i].ap(),
                in_=k.dram_in("rot_main", [4, T, FD])[i]
                .rearrange("(g p) n -> p g n", p=P)), 16)
        x1_fm = k.sbuf("x1_fm", [P, CCH, T + 1], BF16)
        b.barrier(("pool", "pe", "dve", "act"))

        def mkxch(pfx):
            return dict(
                kb_i=k.dram_tmp(pfx + "kb_i", [64, FH * 512], BF16),
                kb_o=k.dram_tmp(pfx + "kb_o", [64, FH * 512], BF16),
                vb_i=k.dram_tmp(pfx + "vb_i", [P, 4 * 512], BF16),
                vb_o=k.dram_tmp(pfx + "vb_o", [P, 4 * 512], BF16),
                pa_i=k.dram_tmp(pfx + "pa_i", [P, CCH], BF16),
                pa_o=k.dram_tmp(pfx + "pa_o", [P, CCH], BF16))

        k.emit_ffn_layer(0, xn_fm, P, x1_fm, 1, rot, "L1", mkxch("l1"))
        k.dbg_dump("x1_fm", x1_fm.ap().rearrange("p c n -> p (c n)"),
                   [P, CCH * (T + 1)], dep=k.b.ms("dve"))
    if stage_limit >= 10:
        x2v = xn_fm.ap()[:, :, 0:T + 1]
        b.barrier()
        x1c_i = k.dram_tmp("x1c_i", [P, CCH], BF16)
        x1c_o = k.dram_tmp("x1c_o", [P, CCH], BF16)
        x1cst = k.sbuf("x1c_stage", [P, CCH], BF16)
        mg = b.fence("dve", nc.vector.tensor_scalar(
            x1cst.ap().rearrange("p (c o) -> p c o", o=1),
            x1_fm.ap()[:, :, T:T + 1], hginv[:], None, ALU.mult))
        b.wait("sp", mg)
        mxd = b.mark("sp", nc.sync.dma_start(out=x1c_i, in_=x1cst.ap()), 16)
        b.wait("pool", mxd)
        import os as _os
        if _os.environ.get("XCH_X1C", "1") == "1":
            mcc_x1 = b.mark("cc", nc.gpsimd.collective_compute(
                "AllReduce", mybir.AluOpType.add,
                replica_groups=[[0, 1], [2, 3], [4, 5], [6, 7]],
                ins=[x1c_i.opt()], outs=[x1c_o.opt()]), 1)
            b.wait("pool", mcc_x1)
        mxl = b.mark("pool", nc.gpsimd.dma_start(
            out=x1_fm.ap()[:, :, 0:1],
            in_=x1c_o.rearrange("p (c o) -> p c o", o=1),
            allow_slow_non_contiguous=True), 16)
        b.wait("dve", mxl)
        mcol = b.fence("dve", nc.vector.tensor_scalar(
            x1_fm.ap()[:, :, 0:1], x1_fm.ap()[:, :, 0:1],
            hgate[:], None, ALU.mult))
        k.emit_ffn_layer(1, x1_fm, 1, None, 1, rot, "L2", mkxch("l2"),
                         dst_ap=x2v, col0_ms=mcol)
        k.dbg_dump("x2_fm", x2v, [P, CCH, T + 1], dep=k.b.ms("dve"))

    if stage_limit >= 11:
        b.barrier()
        k.sbuf_free_from("f_qa")
        k.emit_final(xn_fm.ap()[:, :, 0:T + 1], out_dram)

    if "s_sc" in k.dbg_names and "s_sc" in k.tmp:
        o = k.dram_out("dbg_s_sc", [T, C], F32)
        b.mark("sp", nc.sync.dma_start(out=o, in_=k.tmp["s_sc"]), 16)
    if "y_sc" in k.dbg_names and "y_sc" in k.tmp:
        o = k.dram_out("dbg_y_sc", [T, C], F32)
        b.mark("pool", nc.gpsimd.dma_start(
            out=o.rearrange("(g p) n -> g p n", p=P), in_=k.tmp["y_sc"]), 16)
    if "k_sc" in k.dbg_names and "k_sc" in k.tmp:
        o = k.dram_out("dbg_k_sc", [CCH, P, T], F32)
        b.mark("pool", nc.gpsimd.dma_start(out=o, in_=k.tmp["k_sc"]), 16)
    if "v_sc" in k.dbg_names and "v_sc" in k.tmp:
        o = k.dram_out("dbg_v_sc", [T // P, P, C], F32)
        b.mark("pool", nc.gpsimd.dma_start(out=o, in_=k.tmp["v_sc"]), 16)
    if "g_sc" in k.dbg_names and "g_sc" in k.tmp:
        o = k.dram_out("dbg_g_sc", [T // P, P, C], F32)
        b.mark("pool", nc.gpsimd.dma_start(out=o, in_=k.tmp["g_sc"]), 16)

    b.barrier(("sp", "pool"))
    return k


def kernel(**inputs):
    cores = list(range(8))
    in_maps = [make_core_inputs(inputs, c) for c in cores]
    k = build()
    used = set(k.inp.keys())
    in_maps = [{n: m[n] for n in used} for m in in_maps]
    res = run_bass_kernel_spmd(k.nc, in_maps, core_ids=cores)
    x = np.asarray(inputs["L0_attn_x"])
    out = np.zeros_like(x)
    for c in cores:
        b, half = c // 2, c % 2
        out[b, half * T:(half + 1) * T] = res.results[c]["out"]
    return out


# version 32
# speedup vs baseline: 1.1743x; 1.1743x over previous
import numpy as np
import ml_dtypes
from contextlib import ExitStack
import concourse.bass as bass
import concourse.mybir as mybir
from concourse.bass_utils import run_bass_kernel_spmd

F32 = mybir.dt.float32
BF16 = mybir.dt.bfloat16
U8 = mybir.dt.uint8
AF = mybir.ActivationFunctionType
ALU = mybir.AluOpType
AX = mybir.AxisListType.X

P = 128
C = 1024
CCH = 8
T = 2048
NG = 16
NGH = 17
LCH = 64
NHEAD = 16
HK = 64
W = 512
FH = 8
FD = 64
INNER = 512
HID = 2730
HIDP = 2816
NEG = -1.0e30

BF = ml_dtypes.bfloat16


class Bld:

    ENGS = ("pe", "dve", "act", "pool", "sp")
    SEMS = ENGS + ("pld", "cc")

    def __init__(self, nc, es):
        self.nc = nc
        self.eng = dict(pe=nc.tensor, dve=nc.vector, act=nc.scalar,
                        pool=nc.gpsimd, sp=nc.sync)
        self.sem = {e: es.enter_context(nc.semaphore(f"sem_{e}")) for e in self.SEMS}
        self.cnt = {e: 0 for e in self.SEMS}
        self.lastw = {e: {e2: 0 for e2 in self.SEMS} for e in self.ENGS}

    def mark(self, e, ins, k=1):
        if e == "pool" and k == 16:
            e = "pld"
        ins.then_inc(self.sem[e], k)
        self.cnt[e] += k
        return (e, self.cnt[e])

    def ms(self, e):
        if e == "pool":
            return [("pool", self.cnt["pool"]), ("pld", self.cnt["pld"])]
        return (e, self.cnt[e])

    def wait(self, e, *mss):
        flat = []
        for m in mss:
            if isinstance(m, list):
                flat.extend(m)
            elif m is not None:
                flat.append(m)
        for (e2, v) in flat:
            if v <= 0:
                continue
            if self.lastw[e][e2] < v:
                self.eng[e].wait_ge(self.sem[e2], v)
                self.lastw[e][e2] = v

    def fence(self, e, ins):
        m = self.mark(e, ins)
        self.wait(e, m)
        return m

    def barrier(self, engs=None):
        engs = engs or self.ENGS
        mss = [self.ms(e) for e in engs]
        for e in engs:
            self.wait(e, *mss)


def _rotary_planes(positions):
    n = len(positions)
    half_idx = np.arange(0, FD, 2, dtype=np.float64)
    inv_freq = 1.0 / 10000.0 ** (half_idx / FD)
    t = positions.astype(np.float64)
    freqs = t[:, None] * inv_freq[None, :]
    freqs = np.concatenate([freqs, freqs], -1)
    base = (half_idx + 0.4 * FD) / (1.4 * FD)
    power = (t - 4096 // 2) / float(W // 2)
    scale = base[None, :] ** power[:, None]
    scale = np.concatenate([scale, scale], -1)
    cos, sin = np.cos(freqs), np.sin(freqs)
    sgn = np.where(np.arange(FD) < 32, -1.0, 1.0)[None, :]
    return (np.ascontiguousarray((8.0 * cos * scale), np.float32),
            np.ascontiguousarray((8.0 * sin * scale * sgn), np.float32),
            np.ascontiguousarray((cos / scale), np.float32),
            np.ascontiguousarray((sin / scale * sgn), np.float32))


def make_core_inputs(inputs, core):
    b, half = core // 2, core % 2
    x = np.ascontiguousarray(np.asarray(inputs["L0_attn_x"], np.float32))
    d = {}

    def halo_block(row127):
        blk = np.zeros((P, C), np.float32)
        if row127 is not None:
            blk[127] = row127
        return blk

    t0 = half * T
    d["xm"] = np.ascontiguousarray(np.concatenate(
        [halo_block(x[b, t0 - 1] if t0 > 0 else None), x[b, t0:t0 + T]], 0))
    if half == 1:
        d["xp"] = np.ascontiguousarray(
            np.concatenate([halo_block(None), x[b, 0:T]], 0))
    else:
        d["xp"] = np.zeros((P + T, C), np.float32)

    for nm in ("Wr", "Wk", "Wv", "Wg", "Wo", "tm_w1", "td_w1"):
        d[nm] = np.ascontiguousarray(np.asarray(inputs["L0_attn_" + nm]))
    d["tm_w2a"] = np.ascontiguousarray(np.concatenate(
        [np.asarray(inputs["L0_attn_tm_w2"]),
         np.asarray(inputs["L0_attn_time_maa_wkvrg"])[:, None, :]], 1))
    d["td_w2a"] = np.ascontiguousarray(np.concatenate(
        [np.asarray(inputs["L0_attn_td_w2"]),
         np.asarray(inputs["L0_attn_time_decay"])[None, :]], 0))
    d["tmx"] = np.ascontiguousarray(
        np.asarray(inputs["L0_attn_time_maa_x"]).reshape(CCH, P).T)
    u = np.asarray(inputs["L0_attn_time_faaaa"]).astype(np.float32)
    ufm = np.zeros((P, CCH), np.float32)
    for h in range(NHEAD):
        ufm[(h % 2) * 64:(h % 2) * 64 + 64, h // 2] = u[h]
    d["u_fm"] = ufm
    d["glnw"] = np.ascontiguousarray(np.broadcast_to(
        np.asarray(inputs["L0_attn_ln_w"], np.float32), (P, C)))
    d["glnb"] = np.ascontiguousarray(np.broadcast_to(
        np.asarray(inputs["L0_attn_ln_b"], np.float32), (P, C)))

    fw = {kk[len("L0_ffn_"):]: np.asarray(v) for kk, v in inputs.items()
          if kk.startswith("L0_ffn_")}
    for l in (0, 1):
        d[f"wqkv{l}"] = np.ascontiguousarray(fw["wqkv"][l])
        d[f"wo{l}"] = np.ascontiguousarray(fw["wo"][l])
        w1 = fw["w1"][l]
        w1p = np.zeros((C, 2 * HIDP), np.float32)
        w1p[:, :HID] = w1[:, :HID]
        w1p[:, HIDP:HIDP + HID] = w1[:, HID:]
        d[f"w1{l}"] = w1p
        b1 = np.asarray(fw["b1"][l], np.float32)
        b1p = np.zeros((2 * HIDP,), np.float32)
        b1p[:HID] = b1[:HID]
        b1p[HIDP:HIDP + HID] = b1[HID:]
        d[f"b1a{l}"] = np.ascontiguousarray(b1p[:HIDP].reshape(HIDP // P, P).T)
        d[f"b1g{l}"] = np.ascontiguousarray(b1p[HIDP:].reshape(HIDP // P, P).T)
        w2 = fw["w2"][l]
        w2p = np.zeros((HIDP, C), np.float32)
        w2p[:HID] = w2
        d[f"w2{l}"] = w2p
        d[f"b2c{l}"] = np.ascontiguousarray(
            np.asarray(fw["b2"][l], np.float32).reshape(CCH, P).T)
        d[f"qs{l}"] = np.ascontiguousarray(np.broadcast_to(
            np.asarray(fw["q_scale"][l], np.float32), (P, FD)))
        d[f"ks{l}"] = np.ascontiguousarray(np.broadcast_to(
            np.asarray(fw["k_scale"][l], np.float32), (P, FD)))
        d[f"flnw{l}"] = np.ascontiguousarray(
            np.asarray(fw["ln_g"][l], np.float32).reshape(CCH, P).T)
        d[f"flnb{l}"] = np.ascontiguousarray(
            np.asarray(fw["ln_b"][l], np.float32).reshape(CCH, P).T)

    d["ident"] = np.eye(P, dtype=np.float32)
    d["eye64st"] = np.ascontiguousarray(np.tile(np.eye(64, dtype=np.float32), (2, 1)))
    o2 = np.zeros((P, 2), np.float32)
    o2[0:64, 0] = 1.0
    o2[64:128, 1] = 1.0
    d["ones2"] = o2
    d["scan_mask"] = (np.arange(P)[:, None] < np.arange(P)[None, :]).astype(np.uint8)
    d["diag_mask"] = np.where(
        np.arange(P)[:, None] >= np.arange(P)[None, :], 0.0, NEG
    ).astype(np.float32)
    d["mask0c"] = np.full((P, 1), 0.0 if half == 1 else NEG, np.float32)
    d["diag_maskT"] = np.where(
        np.arange(P)[:, None] <= np.arange(P)[None, :], 0.0, NEG
    ).astype(BF)
    d["prevmask"] = np.ascontiguousarray(np.where(
        np.arange(896)[None, :] > np.arange(P)[:, None] + 384,
        NEG, 0.0).astype(BF))
    d["halo_gate"] = np.full((P, 1), 1.0 if half == 1 else 0.0, np.float32)
    d["halo_gate_inv"] = np.full((P, 1), 0.0 if half == 1 else 1.0, np.float32)
    pos_main = np.arange(t0, t0 + T)
    d["rot_main"] = np.ascontiguousarray(np.stack(_rotary_planes(pos_main), 0))
    pos_halo = np.arange(W, T)
    d["rot_halo"] = np.ascontiguousarray(np.stack(_rotary_planes(pos_halo), 0))
    return d


class K:
    def __init__(self, dbg=()):
        self.dbg_names = set(dbg)
        self.nc = bass.Bass("TRN2", target_bir_lowering=False)
        self.es = ExitStack()
        self.b = Bld(self.nc, self.es)
        self.inp = {}
        self.out = {}
        self.sb = {}
        self.ps = {}
        self.tmp = {}

    def dram_in(self, name, shape, dtype=F32):
        if name not in self.inp:
            self.inp[name] = self.nc.dram_tensor(
                name, list(shape), dtype, kind="ExternalInput").ap()
        return self.inp[name]

    def dram_out(self, name, shape, dtype=F32):
        self.out[name] = self.nc.dram_tensor(
            name, list(shape), dtype, kind="ExternalOutput").ap()
        return self.out[name]

    def dram_tmp(self, name, shape, dtype=BF16):
        if name not in self.tmp:
            self.tmp[name] = self.nc.dram_tensor(
                name, list(shape), dtype, kind="Internal").ap()
        return self.tmp[name]

    def sbuf(self, name, shape, dtype=BF16):
        if name not in self.sb:
            if not hasattr(self, "_sb_namectr"):
                self._sb_namectr = {}
            n = self._sb_namectr.get(name, 0)
            self._sb_namectr[name] = n + 1
            uniq = name if n == 0 else f"{name}__{n}"
            cm = self.nc.sbuf_tensor(uniq, list(shape), dtype)
            self.sb[name] = cm.__enter__()
            if not hasattr(self, "_sb_cms"):
                self._sb_cms = {}
                self._sb_order = []
            self._sb_cms[name] = cm
            self._sb_order.append(name)
        return self.sb[name]

    def sbuf_free_from(self, watermark):
        if watermark not in self._sb_order:
            return
        idx = self._sb_order.index(watermark)
        for name in reversed(self._sb_order[idx:]):
            self._sb_cms.pop(name).__exit__(None, None, None)
            del self.sb[name]
        self._sb_order = self._sb_order[:idx]

    def psum(self, name, shape, dtype=F32):
        if name not in self.ps:
            self.ps[name] = self.es.enter_context(
                self.nc.psum_tensor(name, list(shape), dtype))
        return self.ps[name]

    def banks(self):
        if not hasattr(self, "_banks"):
            self._banks = [self.psum(f"bank{i}", [P, 512], F32) for i in range(7)]
            self._banks.append(self.psum("bank7", [P, 1024], BF16))
        return self._banks

    def constv(self, value):
        if not hasattr(self, "_consts"):
            self._consts = {}
        if value not in self._consts:
            t = self.sbuf(f"const_{len(self._consts)}", [P, 1], F32)
            self.b.mark("pool", self.nc.gpsimd.memset(t[:], float(value)))
            self._consts[value] = t
        return self._consts[value][:]

    def dbg_dump(self, name, ap_sbuf, shape, dtype=F32, dep=None):
        if name not in self.dbg_names:
            return
        o = self.dram_out("dbg_" + name, shape, F32)
        if dep:
            self.b.wait("pool", dep)
        self.b.mark("pool", self.nc.gpsimd.dma_start(out=o, in_=ap_sbuf), 16)

    def emit_ln_fm(self, x_dram, xn_fm_ap, ngroups):
        nc, b = self.nc, self.b
        xt = self.sbuf("scrF1", [P, C], F32)
        cen = self.sbuf("scrF2", [P, C], F32)
        xnb = self.sbuf("scrB1", [P, C], BF16)
        st = self.sbuf("ln_st", [P, 4], F32)
        tps = self.banks()[7]
        for g in range(ngroups):
            ld = b.mark("sp", nc.sync.dma_start(
                out=xt[:], in_=x_dram[g * P:(g + 1) * P, :]), 16)
            b.wait("dve", ld)
            m0 = b.mark("dve", nc.vector.reduce_sum(st[:, 0:1], xt[:], axis=AX))
            b.wait("dve", m0)
            m1 = b.mark("dve", nc.vector.tensor_scalar_mul(
                st[:, 1:2], st[:, 0:1], 1.0 / C))
            b.wait("dve", m1)
            nc.vector.tensor_scalar(cen[:], xt[:], st[:, 1:2], None, ALU.subtract)
            sq = xt
            nc.vector.tensor_mul(sq[:], cen[:], cen[:])
            m1b = b.mark("dve", nc.vector.reduce_sum(st[:, 2:3], sq[:], axis=AX))
            b.wait("dve", m1b)
            m2 = b.mark("dve", nc.vector.tensor_scalar_mul(
                st[:, 2:3], st[:, 2:3], 1.0 / C))
            b.wait("act", m2)
            eps_ap = self.constv(1e-5)
            if g == 0:
                b.wait("act", self.b.ms("pool"))
            nc.scalar.activation(st[:, 3:4], st[:, 2:3], AF.Ln, bias=eps_ap)
            m3 = b.mark("act", nc.scalar.activation(
                st[:, 3:4], st[:, 3:4], AF.Exp, scale=-0.5))
            b.wait("dve", m3)
            m4 = b.mark("dve", nc.vector.tensor_scalar(
                xnb[:], cen[:], st[:, 3:4], None, ALU.mult))
            b.wait("pe", m4)
            mt = None
            for cc in range(CCH):
                mt = nc.tensor.transpose(tps[:, cc * P:(cc + 1) * P],
                                         xnb[:, cc * P:(cc + 1) * P],
                                         self.sb["ident_bf"][:])
            mt = b.mark("pe", mt)
            b.wait("dve", mt)
            ev = b.mark("dve", nc.vector.tensor_copy(
                xn_fm_ap[:, :, g * P:(g + 1) * P],
                tps.ap().rearrange("p (c n) -> p c n", c=CCH)))
            if g == 3:
                self.dbg_dump("ln_xt", xt[:], [P, C], dep=ld)
                self.dbg_dump("ln_st", st[:], [P, 4], dep=b.ms("act"))
                self.dbg_dump("ln_xnb", xnb[:], [P, C], dep=m4)
            b.wait("sp", ev)
            b.wait("pe", ev)
        return b.ms("dve")


    def load_w_bf(self, bufname, dram_ap, rows, cols):
        kc = rows // P
        sb = self.sbuf(bufname, [P, kc, cols], BF16)
        m = self.b.mark("pool", self.nc.gpsimd.dma_start(
            out=sb.ap()[:, 0:kc, 0:cols],
            in_=dram_ap.rearrange("(k p) n -> p k n", p=P)), 16)
        return sb, m

    def emit_rwkv_mix(self, xn_fm, which, T_, tag):
        nc, b = self.nc, self.b
        ntile = T_ // 512
        IW = {"w": 0, "k": 1, "v": 2, "r": 3, "g": 4}
        tm1, m1ld = self.load_w_bf("wsmall", self.dram_in("tm_w1", [C, 160]), C, 160)
        tm2 = self.sbuf("w_tm2", [33, 5, C], BF16)
        m2ld = b.mark("pool", nc.gpsimd.dma_start(
            out=tm2.ap(), in_=self.dram_in("tm_w2a", [5, 33, C]).rearrange("f a n -> a f n")), 16)
        td1 = self.sbuf("w_td1", [P, CCH, 64], BF16)
        td1ld = b.mark("pool", nc.gpsimd.dma_start(
            out=td1.ap(), in_=self.dram_in("td_w1", [C, 64]).rearrange("(k p) n -> p k n", p=P)), 16)
        td2 = self.sbuf("w_td2", [65, C], BF16)
        td2ld = b.mark("pool", nc.gpsimd.dma_start(
            out=td2.ap(), in_=self.dram_in("td_w2a", [65, C])), 16)
        tmx = self.sbuf("tmx_sb", [P, CCH], F32)
        tmxld = b.mark("pool", nc.gpsimd.dma_start(
            out=tmx.ap(), in_=self.dram_in("tmx", [P, CCH])), 16)

        sx = self.sbuf("mx_sx", [P, CCH, 512], BF16)
        xxx = self.sbuf("mx_xxx", [P, CCH, 512], BF16)
        xi = self.sbuf("mx_xi", [P, CCH, 512], BF16)
        evb = self.sbuf("mx_ev", [P, 512], BF16)
        evb2 = self.sbuf("mx_ev2", [P, 512], BF16)
        mts = [self.sbuf(f"mx_m{f}", [33, T], BF16) for f in range(5)]
        m2a = self.sbuf("mx_m2a", [65, T], BF16)
        B = self.banks()
        psm = B[2][0:32, :]
        psd = B[2][:, :]
        ps2 = B[2][0:64, :]
        pso = B[0][:, :]
        pso2 = B[1][:, :]

        for f in range(5):
            b.mark("pool", nc.gpsimd.memset(mts[f][32:33, 0:T_], 1.0))
        b.mark("pool", nc.gpsimd.memset(m2a[64:65, 0:T_], 1.0))
        ones_ms = b.ms("pool")

        def emit_sx(tt):
            c0 = P + tt * 512
            for cc in range(CCH):
                m = nc.vector.tensor_sub(sx[:, cc, :],
                                         xn_fm.ap()[:, cc, c0 - 1:c0 + 511],
                                         xn_fm.ap()[:, cc, c0:c0 + 512])
            return b.mark("dve", m)

        need_f = sorted({IW[i] for i in which})
        b.wait("pe", m1ld, m2ld, tmxld, ones_ms)
        b.wait("dve", tmxld)
        psmR = [psm, B[3][0:32, :]]
        evA = [None, None]
        rrA = 0
        for tt in range(ntile):
            c0 = P + tt * 512
            b.wait("dve", b.ms("pe"))
            sxm = emit_sx(tt)
            for cc in range(CCH):
                nc.vector.tensor_scalar(xxx[:, cc, :], sx[:, cc, :],
                                        tmx[:, cc:cc + 1], None, ALU.mult)
                m = nc.vector.tensor_add(xxx[:, cc, :], xxx[:, cc, :],
                                         xn_fm.ap()[:, cc, c0:c0 + 512])
            mx = b.mark("dve", m)
            b.wait("pe", mx)
            for f in need_f:
                ia = rrA % 2
                rrA += 1
                pm = psmR[ia]
                if evA[ia] is not None:
                    b.wait("pe", evA[ia])
                for kc in range(CCH):
                    mm = nc.tensor.matmul(pm, tm1[:, kc, 32 * f:32 * f + 32],
                                          xxx.ap().rearrange("p c n -> p (c n)")[:, kc * 512:(kc + 1) * 512],
                                          start=(kc == 0), stop=(kc == CCH - 1))
                mms = b.mark("pe", mm)
                b.wait("act", mms)
                ev = b.mark("act", nc.scalar.activation(
                    mts[f][0:32, tt * 512:(tt + 1) * 512], pm, AF.Tanh))
                evA[ia] = ev
        b.wait("pe", b.ms("act"))

        def fold_tile(i, tt):
            f = IW[i]
            c0 = P + tt * 512
            b.wait("dve", b.ms("pe"))
            b.wait("pool", b.ms("pe"))
            sxm = emit_sx(tt)
            for cc in range(CCH):
                mm = nc.tensor.matmul(psd, tm2[:, f, cc * P:(cc + 1) * P],
                                      mts[f][:, tt * 512:(tt + 1) * 512],
                                      start=True, stop=True)
                mms = b.mark("pe", mm)
                b.wait("dve", mms, sxm)
                q = b.mark("dve", nc.vector.tensor_mul(
                    xi[:, cc, :], psd, sx[:, cc, :]))
                b.wait("pe", q)
                b.wait("pool", q)
                b.mark("pool", nc.gpsimd.tensor_add(
                    xi[:, cc, :], xi[:, cc, :], xn_fm.ap()[:, cc, c0:c0 + 512]))
            return b.ms("pool")

        for i in which:
            if i == "w":
                b.wait("pe", td1ld, td2ld)
                for tt in range(ntile):
                    fm = fold_tile(i, tt)
                    b.wait("pe", fm)
                    for kc in range(CCH):
                        mm = nc.tensor.matmul(ps2, td1[:, kc, :],
                                              xi.ap().rearrange("p c n -> p (c n)")[:, kc * 512:(kc + 1) * 512],
                                              start=(kc == 0), stop=(kc == CCH - 1))
                    mms = b.mark("pe", mm)
                    b.wait("act", mms)
                    ev = b.mark("act", nc.scalar.activation(
                        m2a[0:64, tt * 512:(tt + 1) * 512], ps2, AF.Tanh))
                    b.wait("pe", ev)
            elif i in ("k", "r"):
                b.wait("pool", b.ms("pe"))
                wsb, wld = self.load_w_bf("wbufA" if i == "k" else "wbufA",
                                          self.dram_in("W" + i, [C, C]), C, C)
                dst = self.dram_tmp(i + "_sc", [CCH, P, T], BF16)
                b.wait("pe", wld)
                PS4 = [pso, pso2, B[3][:, :], B[4][:, :]]
                mxev = [None] * 4
                ebr = [None, None]
                rrm = 0
                for tt in range(ntile):
                    fm = fold_tile(i, tt)
                    b.wait("pe", fm)
                    for oc in range(CCH):
                        ii = rrm % 4
                        rrm += 1
                        ps = PS4[ii]
                        eb = evb if oc % 2 == 0 else evb2
                        if mxev[ii] is not None:
                            b.wait("pe", mxev[ii])
                        for kc in range(CCH):
                            mm = nc.tensor.matmul(
                                ps, wsb[:, kc, oc * P:(oc + 1) * P],
                                xi.ap().rearrange("p c n -> p (c n)")[:, kc * 512:(kc + 1) * 512],
                                start=(kc == 0), stop=(kc == CCH - 1))
                        mms = b.mark("pe", mm)
                        b.wait("dve", mms)
                        if ebr[oc % 2] is not None:
                            b.wait("dve", ebr[oc % 2])
                        ev = b.mark("dve", nc.vector.tensor_copy(eb[:], ps))
                        mxev[ii] = ev
                        b.wait("sp", ev)
                        ebr[oc % 2] = b.mark("sp", nc.sync.dma_start(
                            out=dst[oc, :, tt * 512:(tt + 1) * 512], in_=eb[:]), 16)
            elif i in ("v", "g"):
                b.wait("pool", b.ms("pe"))
                wsb, wld = self.load_w_bf("wbufB",
                                          self.dram_in("W" + i, [C, C]), C, C)
                dst = self.dram_tmp(i + "_sc", [T // P, P, C], BF16)
                act = AF.Silu if i == "g" else None
                b.wait("pe", wld)
                PS4 = [pso, pso2, B[3][:, :], B[4][:, :]]
                mxev = [None] * 4
                ebr = [None, None]
                rrm = 0
                eng = "act" if act else "dve"
                for tt in range(ntile):
                    fm = fold_tile(i, tt)
                    b.wait("pe", fm)
                    for gg in range(4):
                        grp = tt * 4 + gg
                        for half in range(2):
                            ii = rrm % 4
                            rrm += 1
                            ps = PS4[ii]
                            eb = evb if half == 0 else evb2
                            if mxev[ii] is not None:
                                b.wait("pe", mxev[ii])
                            for kc in range(CCH):
                                mm = nc.tensor.matmul(
                                    ps, xi[:, kc, gg * P:(gg + 1) * P],
                                    wsb[:, kc, half * 512:(half + 1) * 512],
                                    start=(kc == 0), stop=(kc == CCH - 1))
                            mms = b.mark("pe", mm)
                            b.wait(eng, mms)
                            if ebr[half] is not None:
                                b.wait(eng, ebr[half])
                            if act:
                                ev = b.mark("act", nc.scalar.activation(eb[:], ps, act))
                            else:
                                ev = b.mark("dve", nc.vector.tensor_copy(eb[:], ps))
                            mxev[ii] = ev
                            b.wait("sp", ev)
                            ebr[half] = b.mark("sp", nc.sync.dma_start(
                                out=dst[grp, :, half * 512:(half + 1) * 512],
                                in_=eb[:]), 16)
        b.barrier()
        return m2a, td2

    def emit_rwkv_scan(self, tag, T_, m2a, td2, S_init, with_y, save_corr=False):
        nc, b = self.nc, self.b
        import os
        SC_HP = int(os.environ.get("SC_HP", CCH))
        SC_CP = int(os.environ.get("SC_CP", 99))
        LC = P
        NCH = T_ // LC
        et = self.sbuf("sc_e", [P, 2048], F32)
        cums = self.sbuf("sc_cums", [P, 2048 + 1], F32)
        bst = self.sbuf("sc_bst", [P, 17], F32)
        stat = self.sbuf("sc_stat", [P, 3 * 16], F32)
        ksl = self.sbuf("sc_ksl", [P, 2048], BF16)
        rsl = self.sbuf("sc_rsl", [P, 2048], BF16)
        vsl = self.sbuf("sc_vsl", [P, 16, P], BF16)
        ysl2 = [self.sbuf("sc_ysl", [P, P], BF16),
                self.sbuf("sc_ysl2", [P, P], BF16)]
        ysl_ev = [None, None]
        ekh_pl = self.sbuf("sc_ekh", [P, 2048], BF16)
        ek_pl = self.sbuf("sc_ek", [P, 2048], BF16)
        khat_pl = self.sbuf("sc_khp", [P, 2048], BF16)
        rt_pl = self.sbuf("sc_rtp", [P, 2048], BF16)
        rh_pl = self.sbuf("sc_rhp", [P, 2048], BF16)
        arg = self.sbuf("sc_arg", [P, 2048], F32)
        rtf = self.sbuf("sc_rtf", [P, P], BF16)
        rtz = [self.sbuf("sc_rtz0", [P, P], BF16), self.sbuf("sc_rtz1", [P, P], BF16)]
        ktz = [self.sbuf("sc_ktz0", [P, P], BF16), self.sbuf("sc_ktz1", [P, P], BF16)]
        ktok = self.sbuf("sc_ktok", [P, P], BF16)
        am = [self.sbuf("sc_am0", [P, P], BF16), self.sbuf("sc_am1", [P, P], BF16)]
        dg = self.sbuf("sc_dg", [P, P], BF16)
        rku = self.sbuf("sc_rku", [P, P], BF16)
        dd = self.sbuf("sc_dd", [P, 2], F32)
        negone = self.sbuf("sc_no", [P, 2048], BF16)
        Scur = self.sbuf("sc_S", [P, 64], F32)
        Scur_bf = self.sbuf("sc_Sbf", [P, 64], BF16)
        B = self.banks()
        psW = B[2][:, :]
        psA = B[3][:, 0:P]
        psT = B[7][:, 0:P]
        psS = B[4][:, 0:64]
        psY = B[5][:, 0:P]
        psD = B[6][:, 0:2]
        ident = self.sb["ident_bf"]
        ones2 = self.sb["ones2_bf"]
        maskl = self.sb["scanmask_sb"]
        u_sb = self.sb["u_sb"]
        k_sc = self.dram_tmp("k_sc", [CCH, P, T], BF16)
        r_sc = self.dram_tmp("r_sc", [CCH, P, T], BF16)
        v_sc = self.dram_tmp("v_sc", [T // P, P, C], BF16)
        y_sc = self.dram_tmp("y_sc", [T // P, P, C], BF16)
        if save_corr:
            rh_sc = self.dram_tmp("rh_sc", [CCH, P, T], BF16)
            bst_dram = self.dram_tmp("bst_sc", [P, CCH * 16], F32)
        rh_store = None
        bst_store = None

        b.mark("pool", nc.gpsimd.memset(cums[:, 0:1], 0.0))
        b.mark("pool", nc.gpsimd.memset(negone[:], -1.0))
        for z in rtz + ktz:
            b.mark("pool", nc.gpsimd.memset(z[:], 0.0))
        b.wait("dve", b.ms("pool"))

        NA, ND = 16, 32
        for hp in range(min(CCH, SC_HP)):
            b.wait("sp", b.ms("dve"))
            b.wait("dve", b.ms("pe"))
            b.wait("act", b.ms("pe"))
            if rh_store is not None:
                b.wait("dve", rh_store)
            lk = b.mark("sp", nc.sync.dma_start(out=ksl[:, 0:T_], in_=k_sc[hp, :, 0:T_]), 16)
            lv = b.mark("sp", nc.sync.dma_start(
                out=vsl.ap()[:, 0:T_ // P, :], in_=v_sc[0:T_ // P, :, hp * P:(hp + 1) * P]
                .rearrange("g p n -> p g n")), 16)
            if with_y:
                lr = b.mark("sp", nc.sync.dma_start(out=rsl[:, 0:T_], in_=r_sc[hp, :, 0:T_]), 16)
            b.wait("pe", b.ms("dve"))
            b.wait("act", b.ms("dve"))
            for tt in range(T_ // 512):
                mm = nc.tensor.matmul(psW, td2[:, hp * P:(hp + 1) * P],
                                      m2a[:, tt * 512:(tt + 1) * 512],
                                      start=True, stop=True)
                mms = b.mark("pe", mm)
                b.wait("act", mms)
                ev = b.mark("act", nc.scalar.activation(
                    et[:, tt * 512:(tt + 1) * 512], psW, AF.Exp))
                b.wait("pe", ev)
            b.wait("dve", b.ms("act"))
            sc = b.fence("dve", nc.vector.tensor_tensor_scan(
                cums[:, 1:T_ + 1], et[:, 0:T_],
                negone[:, 0:T_], 0.0,
                ALU.subtract, ALU.mult))
            if bst_store is not None:
                b.wait("dve", bst_store)
            b.fence("dve", nc.vector.tensor_copy(
                bst[:, 0:NCH + 1], cums.ap()[:, 0:T_ + 1:LC]))
            if save_corr:
                b.wait("sp", b.ms("dve"))
                bst_store = b.mark("sp", nc.sync.dma_start(
                    out=bst_dram[:, hp * 16:hp * 16 + NCH],
                    in_=bst[:, 0:NCH]), 16)
            b.fence("dve", nc.vector.tensor_add(
                stat[:, 0:NCH], bst[:, 0:NCH], bst[:, 1:NCH + 1]))
            b.fence("dve", nc.vector.tensor_scalar_mul(
                stat[:, 0:NCH], stat[:, 0:NCH], 0.5))
            b.fence("dve", nc.vector.tensor_scalar_mul(
                stat[:, NA:NA + NCH], stat[:, 0:NCH], -1.0))
            mstat = b.fence("dve", nc.vector.tensor_sub(
                stat[:, ND:ND + NCH], bst[:, 1:NCH + 1], bst[:, 0:NCH]))
            b.wait("act", mstat)
            mdt = b.mark("act", nc.scalar.activation(
                stat[:, ND:ND + NCH], stat[:, ND:ND + NCH], AF.Exp))
            b.wait("dve", mdt, lk, lv)
            if with_y:
                b.wait("dve", lr)
            b.fence("dve", nc.vector.tensor_copy(Scur[:, :], S_init.ap()[:, hp, :]))
            b.fence("dve", nc.vector.tensor_copy(Scur_bf[:, :], Scur[:, :]))
            c3 = cums.ap()[:, 1:T_ + 1].rearrange("p (c n) -> p c n", n=LC)
            bb1 = bst.ap()[:, 1:NCH + 1].rearrange("p (c o) -> p c o", o=1)
            mpl = b.fence("dve", nc.vector.tensor_sub(
                arg.ap()[:, 0:T_].rearrange("p (c n) -> p c n", n=LC),
                bb1.broadcast_to([P, NCH, LC]), c3))
            b.wait("act", mpl)
            mpl = b.mark("act", nc.scalar.activation(
                ekh_pl[:, 0:T_], arg[:, 0:T_], AF.Exp))
            b.wait("dve", mpl)
            mkh = b.mark("dve", nc.vector.tensor_mul(
                khat_pl[:, 0:T_], ksl[:, 0:T_], ekh_pl[:, 0:T_]))
            if with_y:
                nc.vector.tensor_scalar(
                    ekh_pl[:, 0:T_], rsl[:, 0:T_], u_sb[:, hp:hp + 1],
                    None, ALU.mult)
                b.mark("dve", nc.vector.tensor_mul(
                    ekh_pl[:, 0:T_], ekh_pl[:, 0:T_], ksl[:, 0:T_]))
            aal = stat.ap()[:, 0:NCH].rearrange("p (c o) -> p c o", o=1)
            mpl = b.fence("dve", nc.vector.tensor_sub(
                arg.ap()[:, 0:T_].rearrange("p (c n) -> p c n", n=LC),
                aal.broadcast_to([P, NCH, LC]), c3))
            b.wait("act", mpl)
            mek = b.mark("act", nc.scalar.activation(
                ek_pl[:, 0:T_], arg[:, 0:T_], AF.Exp))
            if with_y:
                c0v = cums.ap()[:, 0:T_].rearrange("p (c n) -> p c n", n=LC)
                mpl = b.fence("dve", nc.vector.tensor_sub(
                    arg.ap()[:, 0:T_].rearrange("p (c n) -> p c n", n=LC),
                    c0v, aal.broadcast_to([P, NCH, LC])))
                b.wait("act", mpl)
                mer = b.mark("act", nc.scalar.activation(
                    arg[:, 0:T_], arg[:, 0:T_], AF.Exp))
                b.wait("dve", mer)
                mrt = b.fence("dve", nc.vector.tensor_mul(
                    rt_pl[:, 0:T_], rsl[:, 0:T_], arg[:, 0:T_]))
                bb0 = bst.ap()[:, 0:NCH].rearrange("p (c o) -> p c o", o=1)
                mpl = b.fence("dve", nc.vector.tensor_sub(
                    arg.ap()[:, 0:T_].rearrange("p (c n) -> p c n", n=LC),
                    c0v, bb0.broadcast_to([P, NCH, LC])))
                b.wait("act", mpl)
                mer2 = b.mark("act", nc.scalar.activation(
                    arg[:, 0:T_], arg[:, 0:T_], AF.Exp))
                b.wait("dve", mer2)
                mrt = b.mark("dve", nc.vector.tensor_mul(
                    rh_pl[:, 0:T_], rsl[:, 0:T_], arg[:, 0:T_]))
                if save_corr:
                    b.wait("sp", mrt)
                    rh_store = b.mark("sp", nc.sync.dma_start(
                        out=rh_sc[hp, :, 0:T_], in_=rh_pl[:, 0:T_]), 16)
            b.wait("dve", mek)
            if hp == 0 and tag + "_pl" in self.dbg_names:
                self.dbg_dump(tag + "_pl_vsl",
                              vsl.ap().rearrange("p g n -> p (g n)"),
                              [P, 16 * P], dep=b.ms("dve"))
                self.dbg_dump(tag + "_pl_ksl", ksl[:, :], [P, 2048], dep=b.ms("dve"))
                self.dbg_dump(tag + "_pl_cums", cums[:, :], [P, 2049], dep=b.ms("dve"))
                self.dbg_dump(tag + "_pl_bst", bst[:, :], [P, 17], dep=b.ms("dve"))
                self.dbg_dump(tag + "_pl_stat", stat[:, :], [P, 48], dep=b.ms("dve"))
                self.dbg_dump(tag + "_pl_ekh", ekh_pl[:, :], [P, 2048], dep=b.ms("dve"))
                self.dbg_dump(tag + "_pl_ek", ek_pl[:, :], [P, 2048], dep=b.ms("dve"))
                self.dbg_dump(tag + "_pl_khat", khat_pl[:, :], [P, 2048], dep=b.ms("dve"))
            prev_cp = None
            for cc in range(min(NCH, SC_CP)):
                t0 = cc * LC
                b.wait("dve", b.ms("pe"))
                if with_y:
                    nc.vector.tensor_copy(rtz[0][0:64, :], rh_pl[0:64, t0:t0 + P])
                    b.mark("dve", nc.vector.tensor_copy(
                        rtz[1][64:128, :], rh_pl[64:128, t0:t0 + P]))
                nc.vector.tensor_mul(ktz[0][0:64, :], ksl[0:64, t0:t0 + P],
                                     ek_pl[0:64, t0:t0 + P])
                mk = b.mark("dve", nc.vector.tensor_mul(
                    ktz[1][64:128, :], ksl[64:128, t0:t0 + P],
                    ek_pl[64:128, t0:t0 + P]))
                b.wait("pe", mk)
                mt = b.mark("pe", nc.tensor.transpose(psT, khat_pl[:, t0:t0 + P], ident[:]))
                b.wait("dve", mt)
                mtv = b.mark("dve", nc.vector.tensor_copy(ktok[:, :], psT))
                if with_y:
                    md = b.mark("pe", nc.tensor.matmul(
                        psD, ekh_pl[:, t0:t0 + P], ones2[:, :],
                        start=True, stop=True))
                    b.wait("dve", md)
                    mdd = b.fence("dve", nc.vector.tensor_copy(dd[:, :], psD))
                    for hs in range(2):
                        ma = b.mark("pe", nc.tensor.matmul(
                            psA, ktz[hs][:, :], rt_pl[:, t0:t0 + P],
                            start=True, stop=True))
                        mdg = b.mark("dve", nc.vector.tensor_scalar(
                            dg[:, :], ident[:, :], dd[:, hs:hs + 1], None, ALU.mult))
                        b.wait("dve", ma)
                        msel = b.mark("dve", nc.vector.select(
                            am[hs][:, :], maskl.ap(), psA, dg[:, :]))
                        b.wait("pe", msel)
                b.wait("pe", mtv)
                mpe = None
                for hs in range(2):
                    mpe = nc.tensor.matmul(
                        psS[hs * 64:hs * 64 + 64, :],
                        ktok[:, hs * 64:hs * 64 + 64],
                        vsl.ap()[:, cc, hs * 64:hs * 64 + 64],
                        start=True, stop=True, tile_position=(0, hs * 64))
                if with_y:
                    if prev_cp is not None:
                        b.wait("pe", prev_cp)
                    for hs in range(2):
                        nc.tensor.matmul(
                            psY[:, hs * 64:hs * 64 + 64], am[hs][:, :],
                            vsl.ap()[:, cc, hs * 64:hs * 64 + 64],
                            start=True, stop=False)
                        mpe = nc.tensor.matmul(
                            psY[:, hs * 64:hs * 64 + 64], rtz[hs][:, :],
                            Scur_bf[:, :], start=False, stop=True)
                mpe = b.mark("pe", mpe)
                b.wait("dve", mpe)
                mup = b.mark("dve", nc.vector.scalar_tensor_tensor(
                    Scur[:, :], Scur[:, :], stat[:, ND + cc:ND + cc + 1],
                    psS, ALU.mult, ALU.add))
                mupb = b.mark("dve", nc.vector.tensor_copy(Scur_bf[:, :], Scur[:, :]))
                b.wait("pe", mupb)
                if with_y:
                    yp = cc % 2
                    ysl = ysl2[yp]
                    if ysl_ev[yp] is not None:
                        b.wait("dve", ysl_ev[yp])
                    me1 = b.mark("dve", nc.vector.tensor_copy(ysl[:, :], psY))
                    b.wait("sp", me1)
                    ysl_ev[yp] = b.mark("sp", nc.sync.dma_start(
                        out=y_sc[cc, :, hp * P:(hp + 1) * P], in_=ysl[:, :]), 16)
                    prev_cp = me1
                else:
                    prev_cp = b.ms("dve")
            b.fence("dve", nc.vector.tensor_copy(S_init.ap()[:, hp, :], Scur[:, :]))
        b.barrier()
        return b.ms("dve")

    def emit_rwkv_out(self, corr=False):
        nc, b = self.nc, self.b
        if corr:
            rh_sc = self.dram_tmp("rh_sc", [CCH, P, T], BF16)
            rh0 = self.sb["sc_rtp"].ap()[:, 0:CCH * P].rearrange(
                "p (c n) -> p c n", n=P)
            rh1 = self.sb["sc_rhp"].ap()[:, 0:CCH * P].rearrange(
                "p (c n) -> p c n", n=P)
            Sdec = self.sb["sc_rtp"][:, CCH * P:CCH * P + 512]
            S_ex = self.sb["sc_arg"].ap()[:, 512:1024].rearrange(
                "p (c n) -> p c n", c=CCH)
            mz0 = self.b.mark("pool", nc.gpsimd.memset(
                self.sb["sc_rtp"][64:128, 0:CCH * P], 0.0))
            mz1 = self.b.mark("pool", nc.gpsimd.memset(
                self.sb["sc_rhp"][0:64, 0:CCH * P], 0.0))
            self.b.wait("sp", mz0, mz1)
        glnw = self.sb["sc_e"][:, 0:C]
        glnb = self.sb["sc_e"][:, C:2 * C]
        l1 = b.mark("pool", nc.gpsimd.dma_start(out=glnw, in_=self.dram_in("glnw", [P, C])), 16)
        l2 = b.mark("pool", nc.gpsimd.dma_start(out=glnb, in_=self.dram_in("glnb", [P, C])), 16)
        self.b.wait("pool", self.b.ms("pe"))
        wo, wold = self.load_w_bf("wbufA", self.dram_in("Wo", [C, C]), C, C)
        y_sc = self.dram_tmp("y_sc", [T // P, P, C], BF16)
        g_sc = self.dram_tmp("g_sc", [T // P, P, C], BF16)
        s_sc = self.dram_tmp("s_sc", [T, C], F32)
        yt = self.sbuf("scrB1", [P, C], BF16)
        gt = self.sb["sc_ek"][:, 0:C]
        yc = self.sbuf("scrF1", [P, C], F32)
        yn = self.sbuf("scrF2", [P, C], F32)
        yng = self.sb["sc_ek"][:, C:2 * C]
        yngf = self.sb["sc_khp"].ap().rearrange("p (c n) -> p c n", c=CCH)[:, :, 0:P]
        sf = self.sb["sc_cums"][:, 0:C]
        st = self.sb["sc_arg"][:, 0:NHEAD * 4]
        B = self.banks()
        pstr = B[7]
        pso = B[0][:, :]
        pso2 = B[1][:, :]
        S0, S1, S2, S3 = 0, NHEAD, 2 * NHEAD, 3 * NHEAD
        b.wait("dve", l1, l2)
        for g in range(NG):
            yld = b.mark("sp", nc.sync.dma_start(out=yt[:], in_=y_sc[g, :, :]), 16)
            gld = b.mark("sp", nc.sync.dma_start(out=gt, in_=g_sc[g, :, :]), 16)
            if corr:
                rhld = b.mark("sp", nc.sync.dma_start(
                    out=rh0[0:64, :, :],
                    in_=rh_sc[0:CCH, 0:64, g * P:(g + 1) * P]
                    .rearrange("c p n -> p c n")), 16)
                rhld = b.mark("sp", nc.sync.dma_start(
                    out=rh1[64:128, :, :],
                    in_=rh_sc[0:CCH, 64:128, g * P:(g + 1) * P]
                    .rearrange("c p n -> p c n")), 16)
            b.wait("dve", yld)
            if corr:
                psCr = [B[2][:, 0:P], B[3][:, 0:P]]
                psC_ev = [None, None]
                b.wait("pe", rhld)
                for hp in range(CCH):
                    m_ = nc.vector.tensor_scalar(
                        Sdec[:, hp * 64:(hp + 1) * 64], S_ex[:, hp, :],
                        self.sb["sc_arg"][:, 256 + hp * 16 + g:
                                          257 + hp * 16 + g], None, ALU.mult)
                mdq = b.fence("dve", m_)
                b.wait("pe", mdq)
                for hp in range(CCH):
                    ci = hp % 2
                    psC = psCr[ci]
                    if psC_ev[ci] is not None:
                        b.wait("pe", psC_ev[ci])
                    for hs, rhb in ((0, rh0), (1, rh1)):
                        mm = nc.tensor.matmul(
                            psC[:, hs * 64:(hs + 1) * 64],
                            rhb[:, hp, :], Sdec[:, hp * 64:(hp + 1) * 64],
                            start=True, stop=True)
                    mm = b.mark("pe", mm)
                    b.wait("dve", mm)
                    madd = b.mark("dve", nc.vector.tensor_add(
                        yt[:, hp * P:(hp + 1) * P],
                        yt[:, hp * P:(hp + 1) * P], psC))
                    psC_ev[ci] = madd
            y3 = yt.ap().rearrange("p (h n) -> p h n", h=NHEAD)
            b.fence("dve", nc.vector.tensor_reduce(
                st[:, S0:S0 + NHEAD], y3, AX, ALU.add))
            b.fence("dve", nc.vector.tensor_scalar_mul(
                st[:, S1:S1 + NHEAD], st[:, S0:S0 + NHEAD], 1.0 / HK))
            b.fence("dve", nc.vector.tensor_sub(
                yc.ap().rearrange("p (h n) -> p h n", h=NHEAD), y3,
                st[:, S1:S1 + NHEAD].rearrange("p (h o) -> p h o", o=1
                                                    ).broadcast_to([P, NHEAD, HK])))
            nc.vector.tensor_mul(yn[:], yc[:], yc[:])
            b.fence("dve", nc.vector.tensor_reduce(
                st[:, S2:S2 + NHEAD],
                yn.ap().rearrange("p (h n) -> p h n", h=NHEAD), AX, ALU.add))
            m = b.fence("dve", nc.vector.tensor_scalar_mul(
                st[:, S2:S2 + NHEAD], st[:, S2:S2 + NHEAD], 1.0 / HK))
            b.wait("act", m)
            nc.scalar.activation(st[:, S3:S3 + NHEAD], st[:, S2:S2 + NHEAD],
                                 AF.Ln, bias=self.constv(1e-5))
            m = b.mark("act", nc.scalar.activation(
                st[:, S3:S3 + NHEAD], st[:, S3:S3 + NHEAD], AF.Exp, scale=-0.5))
            b.wait("dve", m, gld)
            nc.vector.tensor_mul(
                yn.ap().rearrange("p (h n) -> p h n", h=NHEAD),
                yc.ap().rearrange("p (h n) -> p h n", h=NHEAD),
                st[:, S3:S3 + NHEAD].rearrange("p (h o) -> p h o", o=1
                                                    ).broadcast_to([P, NHEAD, HK]))
            nc.vector.tensor_mul(yn[:], yn[:], glnw)
            nc.vector.tensor_add(yn[:], yn[:], glnb)
            m = b.mark("dve", nc.vector.tensor_mul(yng, yn[:], gt))
            b.wait("pe", m, wold)
            mt = None
            for cc in range(CCH):
                mt = nc.tensor.transpose(pstr[:, cc * P:(cc + 1) * P],
                                         yng[:, cc * P:(cc + 1) * P],
                                         self.sb["ident_bf"][:])
            mt = b.mark("pe", mt)
            b.wait("dve", mt)
            ev = b.mark("dve", nc.vector.tensor_copy(
                yngf, pstr.ap().rearrange("p (c n) -> p c n", c=CCH)))
            b.wait("pe", ev)
            b.wait("sp", ev)
            for half in range(2):
                ps = pso if half == 0 else pso2
                for kc in range(CCH):
                    mm = nc.tensor.matmul(
                        ps, yngf[:, kc, :],
                        wo[:, kc, half * 512:(half + 1) * 512],
                        start=(kc == 0), stop=(kc == CCH - 1))
                mms = b.mark("pe", mm)
                b.wait("dve", mms)
                if g + half >= 1:
                    b.wait("dve", b.ms("sp"))
                ev = b.mark("dve", nc.vector.tensor_copy(
                    sf[:, half * 512:(half + 1) * 512], ps))
                b.wait("pe", ev)
            msf = b.ms("dve")
            b.wait("sp", msf)
            b.mark("sp", nc.sync.dma_start(out=s_sc[g * P:(g + 1) * P, :], in_=sf), 16)
        b.barrier()
        return b.ms("sp")


    def emit_qkv(self, l, xsrc, col0, g0, ngrp, qa, ka, va, kacol0, vgrp0, rot,
                 rotoff, with_q, tag, qcol0=0):
        nc, b = self.nc, self.b
        wq = self.sbuf("f_wqkv", [P, CCH, 3 * INNER], BF16)
        if qa is None:
            qa = self.sbuf("f_qa", [65, FH, 512], BF16)
        qs = self.sbuf("f_qs", [P, FD], F32)
        ks = self.sbuf("f_ks", [P, FD], F32)
        if getattr(self, "_qkv_l", None) != l:
            self._qkv_l = l
            wld = b.mark("pool", nc.gpsimd.dma_start(
                out=wq.ap(), in_=self.dram_in(f"wqkv{l}", [C, 3 * INNER])
                .rearrange("(k p) n -> p k n", p=P)), 16)
            b.mark("pool", nc.gpsimd.dma_start(
                out=qs.ap(), in_=self.dram_in(f"qs{l}", [P, FD])), 16)
            qld = b.mark("pool", nc.gpsimd.dma_start(
                out=ks.ap(), in_=self.dram_in(f"ks{l}", [P, FD])), 16)
            self._qkv_ld = (wld, qld)
        wld, qld = self._qkv_ld
        scr = self.sbuf("f_scr", [P, 512], F32)
        nrm = self.sbuf("f_nrm", [P, 24], F32)
        qrt = self.sbuf("f_qrt", [P, 512], BF16)
        qsw = self.sbuf("f_qsw", [P, 512], BF16)
        tmp = self.sbuf("f_rh", [P, 512], BF16)
        B = self.banks()
        ps = B[0][:, :]
        ps2 = B[1][:, :]
        psT = B[7][:, 0:P]
        ident = self.sb["ident_bf"]
        b.wait("pe", wld, qld)

        def project(gi, which, psb):
            c0 = col0 + gi * P
            for kc in range(CCH):
                xs = xsrc.ap()[:, kc, c0:c0 + P] if kc < 4 else \
                    xsrc.ap()[:, kc, c0 - 1:c0 - 1 + P]
                mm = nc.tensor.matmul(psb, xs, wq[:, kc, which * 512:(which + 1) * 512],
                                      start=(kc == 0), stop=(kc == CCH - 1))
            return b.mark("pe", mm)

        def lnrot(g, which, psb, dst_sbuf):
            cq, sq_ = (rot[0], rot[1]) if which == 0 else (rot[2], rot[3])
            sc_ = qs if which == 0 else ks
            b.wait("act", b.ms("pe"))
            msq = b.mark("act", nc.scalar.activation(scr[:], psb, AF.Square))
            b.wait("dve", msq)
            m = b.fence("dve", nc.vector.tensor_reduce(
                nrm[:, 0:FH], scr.ap().rearrange("p (h n) -> p h n", h=FH),
                AX, ALU.add))
            b.wait("act", m)
            nc.scalar.activation(nrm[:, FH:2 * FH], nrm[:, 0:FH], AF.Ln,
                                 bias=self.constv(1e-30))
            m = b.mark("act", nc.scalar.activation(
                nrm[:, FH:2 * FH], nrm[:, FH:2 * FH], AF.Exp, scale=0.5))
            b.wait("dve", m)
            b.fence("dve", nc.vector.tensor_scalar_max(
                nrm[:, FH:2 * FH], nrm[:, FH:2 * FH], 1e-12))
            b.fence("dve", nc.vector.reciprocal(
                nrm[:, 2 * FH:3 * FH], nrm[:, FH:2 * FH]))
            nc.vector.tensor_mul(
                scr.ap().rearrange("p (h n) -> p h n", h=FH),
                psb.rearrange("p (h n) -> p h n", h=FH),
                nrm.ap()[:, 2 * FH:3 * FH].rearrange("p (h o) -> p h o", o=1)
                .broadcast_to([P, FH, FD]))
            b.mark("dve", nc.vector.tensor_mul(
                scr.ap().rearrange("p (h n) -> p h n", h=FH),
                scr.ap().rearrange("p (h n) -> p h n", h=FH),
                sc_.ap().rearrange("p (o n) -> p o n", o=1).broadcast_to([P, FH, FD])))
            rg = rotoff + g
            s3 = scr.ap().rearrange("p (h a n) -> p h a n", h=FH, a=2)
            c4 = cq.ap()[:, rg, :].rearrange("p (o a n) -> p o a n", o=1, a=2)
            s4 = sq_.ap()[:, rg, :].rearrange("p (o a n) -> p o a n", o=1, a=2)
            d3 = dst_sbuf.ap().rearrange("p (h a n) -> p h a n", h=FH, a=2)
            if self._ho_rel is not None:
                b.wait("dve", self._ho_rel)
            nc.vector.tensor_mul(d3, s3, c4.broadcast_to([P, FH, 2, 32]))
            t3 = tmp.ap().rearrange("p (h a n) -> p h a n", h=FH, a=2)
            nc.vector.tensor_mul(
                t3[:, :, 0, :], s3[:, :, 1, :],
                s4[:, :, 0, :].broadcast_to([P, FH, 32]))
            b.mark("dve", nc.vector.tensor_mul(
                t3[:, :, 1, :], s3[:, :, 0, :],
                s4[:, :, 1, :].broadcast_to([P, FH, 32])))
            return b.mark("dve", nc.vector.tensor_add(
                dst_sbuf[:], dst_sbuf[:], tmp[:]))

        pstr = B[7]

        def heads_out(dst, dstcol):
            q4 = qrt.ap().rearrange("p (h a) -> p h a", a=P)
            w4 = qsw.ap().rearrange("p (h a) -> p h a", a=P)
            if self._ho_rel is not None:
                b.wait("dve", self._ho_rel)
            nc.vector.tensor_copy(w4[:, :, 0:64], q4[:, :, 64:128])
            msw = b.fence("dve", nc.vector.tensor_copy(w4[:, :, 64:128], q4[:, :, 0:64]))
            b.wait("pe", msw)
            for hp in range(4):
                for par in range(2):
                    srcap = (qrt if par == 0 else qsw)[:, hp * P:(hp + 1) * P]
                    mt = b.mark("pe", nc.tensor.transpose(psT, srcap, ident[:]))
                    b.wait("dve", mt)
                    mtv = b.mark("dve", nc.vector.tensor_copy(
                        dst.ap()[0:64, 2 * hp + par, dstcol:dstcol + P],
                        psT[0:64, :]))
                    b.wait("pe", mtv)
            self._ho_rel = b.ms("pe")

        self._t7 = [None] * 4
        self._ho_rel = None
        vb_ = [ps, B[2][:, :]]
        kb_ = [ps2, B[3][:, :]]
        qb_ = [B[4][:, :], B[5][:, :]]
        rel = {}
        for gi in range(ngrp):
            g = g0 + gi
            pa = gi % 2
            if rel.get(("v", pa)) is not None:
                b.wait("pe", rel[("v", pa)])
            mv = project(gi, 2, vb_[pa])
            b.wait("dve", mv)
            mev = b.mark("dve", nc.vector.tensor_copy(
                va.ap()[:, (vgrp0 + gi), :], vb_[pa]))
            rel[("v", pa)] = mev
            if rel.get(("k", pa)) is not None:
                b.wait("pe", rel[("k", pa)])
            project(gi, 1, kb_[pa])
            rel[("k", pa)] = lnrot(g, 1, kb_[pa], qrt)
            heads_out(ka, kacol0 + gi * P)
            if with_q:
                if rel.get(("q", pa)) is not None:
                    b.wait("pe", rel[("q", pa)])
                project(gi, 0, qb_[pa])
                rel[("q", pa)] = lnrot(g, 0, qb_[pa], qrt)
                heads_out(qa, qcol0 + gi * P)


    def emit_attention(self, qa, ka, va, w, mask_prev, o_fm, tag,
                       slots=None, qbase=0):
        nc, b = self.nc, self.b
        Pt = self.sbuf("f_pt", [P, 8, 512], BF16)
        nm = self.sbuf("f_nm", [P, 4], BF16)
        rp0 = self.sbuf("f_rp0", [1, 512], BF16)
        rp1 = self.sbuf("f_rp1", [1, 512], BF16)
        rp32 = self.sbuf("f_rp32", [1, 512], F32)
        rcol = self.sbuf("f_rcol", [P, 2], F32)
        osc = self.sbuf("f_osc", [P, 512], BF16)
        otk = self.sbuf("f_otk", [P, P], BF16)
        onescol = self.sbuf("f_ones1", [P, 1], BF16)
        ones11 = self.sbuf("f_ones11", [1, 1], BF16)
        dmaskT = self.sbuf("f_dmaskT", [P, P], BF16)
        mask0 = self.sbuf("f_mask0", [P, 1], F32)
        pmsb = self.sbuf("f_prevmask", [P, 896], BF16)
        if not getattr(self, "_attn_consts", False):
            self._attn_consts = True
            b.mark("pool", nc.gpsimd.memset(onescol[:], 1.0))
            b.mark("pool", nc.gpsimd.memset(ones11[:], 1.0))
            b.mark("pool", nc.gpsimd.dma_start(
                out=dmaskT.ap(),
                in_=self.dram_in("diag_maskT", [P, P], BF16)), 16)
            b.mark("pool", nc.gpsimd.dma_start(
                out=pmsb.ap(), in_=self.dram_in("prevmask", [P, 896], BF16)), 16)
            mc = b.mark("pool", nc.gpsimd.dma_start(
                out=mask0.ap(), in_=self.dram_in("mask0c", [P, 1])), 16)
            b.wait("pe", mc)
            b.wait("dve", mc)
            b.wait("act", mc)
        B = self.banks()
        psc = [B[0][:, :], B[1][:, :]]
        psQp = B[2][:, :]
        psQo = B[3][:, :]
        psO = B[4][:, :]
        psSum = B[5][0:1, 0:512]
        psRc = B[6][:, 0:2]
        psRow = B[6][0:1, 128:256]
        psT = B[7][:, 0:P]
        ident = self.sb["ident_bf"]

        if slots is None:
            kprev = (w % 3) * 512
            kown = ((w + 1) % 3) * 512
            vb_prev = (((w - 1) * 4) % 8, ((w - 1) * 4 + 1) % 8,
                       ((w - 1) * 4 + 2) % 8, ((w - 1) * 4 + 3) % 8)
            vb_own = ((w * 4) % 8, (w * 4 + 1) % 8,
                      (w * 4 + 2) % 8, (w * 4 + 3) % 8)
        else:
            kprev = slots[0] * 512
            kown = slots[1] * 512
            vb_prev = tuple(slots[2] + i for i in range(4))
            vb_own = tuple(slots[3] + i for i in range(4))
        vgrp = list(vb_prev) + list(vb_own)
        b.wait("pool", b.ms("act"))
        for bb_ in range(1, 4):
            b.mark("pool", nc.gpsimd.memset(Pt[:, 4 + bb_, 0:bb_ * P], 0.0))
        mz = b.ms("pool")
        b.wait("act", mz)
        psc4 = [B[0][:, :], B[1][:, :], B[2][:, :], B[3][:, :]]
        psc_ev = [None] * 4
        rrc = [0]
        b6_ev = [None]

        def next_bank():
            i = rrc[0] % 4
            rrc[0] += 1
            if psc_ev[i] is not None:
                b.wait("pe", psc_ev[i])
            return i

        for h in range(FH):
            mq = None
            for p in range(4):
                qsl = qa.ap()[0:64, h, qbase + p * P:qbase + (p + 1) * P]
                now = (p + 1) * P
                ia = next_bank()
                ib = next_bank()
                nc.tensor.matmul(psc4[ia], qsl,
                                 ka.ap()[0:64, h, kprev:kprev + 512],
                                 start=True, stop=True)
                mp = b.mark("pe", nc.tensor.matmul(
                    psc4[ib][:, 0:now], qsl,
                    ka.ap()[0:64, h, kown:kown + now],
                    start=True, stop=True))
                b.wait("dve", mp)
                psc_ev[ia] = b.fence("dve", nc.vector.reduce_max(
                    nm[:, 0:1], psc4[ia], AX, negate=True))
                psc_ev[ib] = b.fence("dve", nc.vector.reduce_max(
                    nm[:, 1:2], psc4[ib][:, 0:now], AX, negate=True))
                mnm = b.mark("dve", nc.vector.tensor_tensor(
                    nm[:, 2:3], nm[:, 0:1], nm[:, 1:2], ALU.min))
                b.wait("pe", mnm)
                if b6_ev[0] is not None:
                    b.wait("pe", b6_ev[0])
                mt = b.mark("pe", nc.tensor.matmul(
                    psRow, nm[:, 2:3], ident[:], start=True, stop=True))
                b.wait("dve", mt)
                mq = b.mark("dve", nc.vector.tensor_copy(
                    qa.ap()[64:65, h, qbase + p * P:qbase + (p + 1) * P], psRow))
                b6_ev[0] = mq
            b.wait("pe", mq)
            for kb in range(8):
                i = next_bank()
                pb = psc4[i]
                own = kb >= 4
                bb_ = kb - 4
                kcc = (kprev + kb * P) if kb < 4 else (kown + (kb - 4) * P)
                mm = b.mark("pe", nc.tensor.matmul(
                    pb, ka.ap()[0:65, h, kcc:kcc + P],
                    qa.ap()[0:65, h, qbase:qbase + 512],
                    start=True, stop=True))
                cA = bb_ * P if own and bb_ > 0 else 0
                md = None
                if own:
                    b.wait("dve", mm)
                    md = b.mark("dve", nc.vector.tensor_add(
                        pb[:, bb_ * P:(bb_ + 1) * P],
                        pb[:, bb_ * P:(bb_ + 1) * P], dmaskT[:]))
                else:
                    b.wait("dve", mm)
                    c0m = kb * P
                    md = b.mark("dve", nc.vector.tensor_add(
                        pb[:, c0m:512], pb[:, c0m:512],
                        pmsb[:, 384:896 - kb * P]))
                    if mask_prev:
                        md = b.mark("dve", nc.vector.tensor_scalar(
                            pb, pb, mask0[:, 0:1], None, ALU.add))
                b.wait("act", mm)
                if md:
                    b.wait("act", md)
                me = b.mark("act", nc.scalar.activation(
                    Pt[:, kb, cA:512], pb[:, cA:512], AF.Exp))
                psc_ev[i] = me
            b.wait("pe", b.ms("act"))
            for kb in range(8):
                ms = nc.tensor.matmul(psSum, onescol[:], Pt[:, kb, :],
                                      start=(kb == 0), stop=(kb == 7))
            ms = b.mark("pe", ms)
            b.wait("dve", ms)
            b.fence("dve", nc.vector.reciprocal(rp32[0:1, :], psSum))
            mr = b.fence("dve", nc.vector.tensor_copy(
                (rp0 if h % 2 == 0 else rp1)[0:1, :], rp32[0:1, :]))
            b.wait("pe", mr)
            hb = (h % 2) * 64
            for kb in range(8):
                mo = nc.tensor.matmul(
                    psO[hb:hb + 64, :],
                    va.ap()[:, vgrp[kb], h * 64:(h + 1) * 64],
                    Pt[:, kb, :], start=(kb == 0), stop=(kb == 7),
                    tile_position=(0, hb))
            mo = b.mark("pe", mo)
            if h % 2 == 1:
                hp = h // 2
                b.wait("dve", mo)
                mev = b.fence("dve", nc.vector.tensor_copy(osc[:], psO))
                b.wait("pe", mev)
                for p in range(4):
                    nc.tensor.matmul(psRc[:, 0:1], rp0[0:1, p * P:(p + 1) * P],
                                     ones11[:], start=True, stop=True)
                    mrc = b.mark("pe", nc.tensor.matmul(
                        psRc[:, 1:2], rp1[0:1, p * P:(p + 1) * P],
                        ones11[:], start=True, stop=True))
                    b.wait("dve", mrc)
                    mrcv = b.fence("dve", nc.vector.tensor_copy(rcol[:], psRc))
                    b.wait("pe", mrcv)
                    mt = b.mark("pe", nc.tensor.transpose(
                        psT, osc[:, p * P:(p + 1) * P], ident[:]))
                    b.wait("dve", mt)
                    nc.vector.tensor_scalar(otk[:, 0:64], psT[:, 0:64],
                                            rcol[:, 0:1], None, ALU.mult)
                    mts = b.fence("dve", nc.vector.tensor_scalar(
                        otk[:, 64:128], psT[:, 64:128],
                        rcol[:, 1:2], None, ALU.mult))
                    b.wait("pe", mts)
                    mt2 = b.mark("pe", nc.tensor.transpose(psT, otk[:], ident[:]))
                    b.wait("dve", mt2)
                    mo2 = b.mark("dve", nc.vector.tensor_copy(
                        o_fm.ap()[:, hp, p * P:(p + 1) * P], psT))
                    b.wait("pe", mo2)

    def emit_wo_resid(self, l, o_fm, xsrc, xcol, dst, dcol, tag):
        nc, b = self.nc, self.b
        wo = self.sbuf("f_wo", [P, 4, C], BF16)
        key = (l, tag)
        if getattr(self, "_wo_l", None) != key:
            self._wo_l = key
            b.wait("pool", b.ms("pe"))
            self._wo_ld = b.mark("pool", nc.gpsimd.dma_start(
                out=wo.ap(), in_=self.dram_in(f"wo{l}", [INNER, C])
                .rearrange("(k p) n -> p k n", p=P)), 16)
        b.wait("pe", self._wo_ld)
        B = self.banks()
        wo_ev = [None, None, None]
        for oc in range(CCH):
            i = oc % 3
            ps = B[i][:, :]
            if wo_ev[i] is not None:
                b.wait("pe", wo_ev[i])
            for kc in range(4):
                mm = nc.tensor.matmul(ps, wo[:, kc, oc * P:(oc + 1) * P],
                                      o_fm.ap()[:, kc, 0:512],
                                      start=(kc == 0), stop=(kc == 3))
            mm = b.mark("pe", mm)
            b.wait("dve", mm)
            ev = b.mark("dve", nc.vector.tensor_add(
                dst.ap()[:, oc, dcol:dcol + 512], ps,
                xsrc.ap()[:, oc, xcol:xcol + 512]))
            wo_ev[i] = ev
        b.wait("pe", b.ms("dve"))

    def emit_ff(self, l, stream, col0, ntok, tag, stash_dram=None,
                stash_ms=None):
        nc, b = self.nc, self.b
        TL = 512 if ntok % 512 == 0 else 256
        w1b = [self.sbuf("f_w1a", [P, CCH, 512], BF16),
               self.sbuf("f_w1b", [P, CCH, 512], BF16)]
        w2b = [self.sbuf("f_w2a", [P, 22, P], BF16),
               self.sbuf("f_w2b", [P, 22, P], BF16)]
        b1a = self.sbuf("f_b1a", [P, 22], F32)
        b1g = self.sbuf("f_b1g", [P, 22], F32)
        b2c = self.sbuf("f_b2c", [P, CCH], F32)
        onesrow = self.sbuf("f_onesr", [1, P], BF16)
        lnwc = self.sbuf("f_lnw", [P, CCH], F32)
        lnbc = self.sbuf("f_lnb", [P, CCH], F32)
        hs = self.sbuf("f_hs", [P, CCH, 512], BF16)
        hstash = self.sbuf("f_hst", [P, CCH], BF16)
        sq = self.sbuf("f_sq", [P, 512], BF16)
        lno = self.sbuf("f_lno", [P, CCH, 512], BF16)
        ffin = self.sbuf("f_ffin", [P, 22, 512], BF16)
        gg = self.sbuf("f_gg", [P, 512], BF16)
        scr1 = self.sbuf("f_scrf", [P, 512], F32)
        murow = self.sbuf("f_mur", [1, 2 * 512], BF16)
        vrow = self.sbuf("f_vr", [1, 512], F32)
        onescol = self.sbuf("f_ones1c", [P, 1], BF16)
        key = (l, tag)
        if getattr(self, "_ff_l", None) != key:
            self._ff_l = key
            b.wait("pool", b.ms("pe"))
            b.mark("pool", nc.gpsimd.memset(onesrow[:], 1.0))
            b.mark("pool", nc.gpsimd.memset(onescol[:], 1.0))
            b.mark("pool", nc.gpsimd.dma_start(
                out=b1a.ap(), in_=self.dram_in(f"b1a{l}", [P, 22])), 16)
            b.mark("pool", nc.gpsimd.dma_start(
                out=b1g.ap(), in_=self.dram_in(f"b1g{l}", [P, 22])), 16)
            b.mark("pool", nc.gpsimd.dma_start(
                out=b2c.ap(), in_=self.dram_in(f"b2c{l}", [P, CCH])), 16)
            b.mark("pool", nc.gpsimd.dma_start(
                out=lnwc.ap(), in_=self.dram_in(f"flnw{l}", [P, CCH])), 16)
            cld = b.mark("pool", nc.gpsimd.dma_start(
                out=lnbc.ap(), in_=self.dram_in(f"flnb{l}", [P, CCH])), 16)
            self._ff_ld = cld
        b.wait("pe", self._ff_ld)
        b.wait("dve", self._ff_ld)
        w1d = self.dram_in(f"w1{l}", [C, 2 * HIDP])
        w2d = self.dram_in(f"w2{l}", [HIDP, C])
        B = self.banks()
        psA_ = [B[0][:, :], B[1][:, :]]
        psS1 = B[2][0:1, 0:TL]
        psS2 = B[3][0:1, 0:TL]
        psMu = B[4][:, 0:TL]
        psRs = B[5][:, 0:TL]
        psA_ = [B[0][:, 0:TL], B[1][:, 0:TL]]
        ntile = ntok // TL
        psR = [B[0][:, 0:TL], B[1][:, 0:TL], B[6][:, 0:TL]]
        bank_ev = [None, None, None]
        wb_ev = {}
        gg_ev = [None, None]
        ggb = [gg, self.sbuf("f_gg2", [P, 512], BF16)]
        rr = 0
        b.wait("dve", b.ms("pe"))
        if stash_dram is None:
            b.fence("dve", nc.vector.tensor_copy(
                hstash.ap().rearrange("p (c o) -> p c o", o=1),
                stream.ap()[:, :, col0 - 1:col0]))
        else:
            if stash_ms is not None:
                b.wait("sp", stash_ms)
            mld = b.mark("sp", nc.sync.dma_start(
                out=hstash.ap(), in_=stash_dram), 16)
            b.wait("dve", mld)
            b.fence("dve", nc.vector.tensor_scalar(
                hstash.ap(), hstash.ap(), self.sb["hgate"][:], None, ALU.mult))
        for t in range(ntile):
            c0 = col0 + t * TL
            b.wait("dve", b.ms("pe"))
            for cc in range(CCH):
                if cc < 4:
                    nc.vector.tensor_copy(hs[:, cc, 0:TL], stream.ap()[:, cc, c0:c0 + TL])
                else:
                    nc.vector.tensor_copy(hs[:, cc, 0:1], hstash[:, cc:cc + 1])
                    nc.vector.tensor_copy(hs[:, cc, 1:TL],
                                          stream.ap()[:, cc, c0:c0 + TL - 1])
            mh = b.fence("dve", nc.vector.tensor_copy(
                hstash.ap().rearrange("p (c o) -> p c o", o=1),
                stream.ap()[:, :, c0 + TL - 1:c0 + TL]))
            b.wait("act", mh)
            b.wait("pe", mh)
            for cc in range(CCH):
                msq = b.mark("act", nc.scalar.activation(
                    sq[:, 0:TL], hs[:, cc, 0:TL], AF.Square))
                b.wait("pe", msq)
                nc.tensor.matmul(psS1, onescol[:], hs[:, cc, 0:TL],
                                 start=(cc == 0), stop=(cc == CCH - 1))
                mpe = b.mark("pe", nc.tensor.matmul(
                    psS2, onescol[:], sq[:, 0:TL],
                    start=(cc == 0), stop=(cc == CCH - 1)))
                b.wait("act", mpe)
            b.wait("dve", b.ms("pe"))
            b.fence("dve", nc.vector.tensor_scalar_mul(
                murow[0:1, 0:TL], psS1, 1.0 / C))
            b.fence("dve", nc.vector.tensor_scalar_mul(
                vrow[0:1, 0:TL], psS2, 1.0 / C))
            nc.vector.tensor_copy(scr1[0:1, 0:TL], murow[0:1, 0:TL])
            nc.vector.tensor_mul(
                scr1[0:1, 0:TL], scr1[0:1, 0:TL], scr1[0:1, 0:TL])
            m = b.mark("dve", nc.vector.tensor_sub(
                vrow[0:1, 0:TL], vrow[0:1, 0:TL], scr1[0:1, 0:TL]))
            b.wait("act", m)
            nc.scalar.activation(vrow[0:1, 0:TL], vrow[0:1, 0:TL], AF.Ln,
                                 bias=self.constv(1e-5)[0:1, :])
            m = b.mark("act", nc.scalar.activation(
                vrow[0:1, 0:TL], vrow[0:1, 0:TL], AF.Exp, scale=-0.5))
            b.wait("dve", m)
            m = b.fence("dve", nc.vector.tensor_copy(
                murow[0:1, 512:512 + TL], vrow[0:1, 0:TL]))
            b.wait("pe", m)
            nc.tensor.matmul(psMu, onesrow[0:1, :], murow[0:1, 0:TL],
                             start=True, stop=True)
            mbc = b.mark("pe", nc.tensor.matmul(
                psRs, onesrow[0:1, :], murow[0:1, 512:512 + TL],
                start=True, stop=True))
            b.wait("dve", mbc)
            for cc in range(CCH):
                nc.vector.scalar_tensor_tensor(
                    scr1[:, 0:TL], hs[:, cc, 0:TL], 1.0, psMu, ALU.mult,
                    ALU.subtract)
                nc.vector.tensor_mul(scr1[:, 0:TL], scr1[:, 0:TL], psRs)
                m = b.fence("dve", nc.vector.tensor_scalar(
                    lno[:, cc, 0:TL], scr1[:, 0:TL], lnwc[:, cc:cc + 1],
                    lnbc[:, cc:cc + 1], ALU.mult, ALU.add))
            b.wait("pe", m)
            if t == 0 and tag + "_lno" in self.dbg_names:
                self.dbg_dump(tag + "_lno", lno.ap().rearrange("p c n -> p (c n)"),
                              [P, CCH * TL], dep=b.ms("dve"))
            for half in range(2):
                for c5 in range((HIDP + 511) // 512):
                    pw = min(512, HIDP - c5 * 512)
                    bi = c5 % 2
                    buf = w1b[bi]
                    if wb_ev.get(("w1", bi)) is not None:
                        b.wait("pool", wb_ev[("w1", bi)])
                    wld = b.mark("pool", nc.gpsimd.dma_start(
                        out=buf.ap()[:, :, 0:pw],
                        in_=w1d[:, half * HIDP + c5 * 512:half * HIDP + c5 * 512 + pw]
                        .rearrange("(k p) n -> p k n", p=P)), 16)
                    b.wait("pe", wld)
                    for oc in range(pw // P):
                        ochunk = c5 * 4 + oc
                        if ochunk >= 22:
                            continue
                        i = rr % 3
                        rr += 1
                        ps = psR[i]
                        if bank_ev[i] is not None:
                            b.wait("pe", bank_ev[i])
                        for kc in range(CCH):
                            mm = nc.tensor.matmul(
                                ps, buf[:, kc, oc * P:(oc + 1) * P],
                                lno[:, kc, 0:TL],
                                start=(kc == 0), stop=(kc == CCH - 1))
                        mm = b.mark("pe", mm)
                        if half == 0:
                            b.wait("dve", mm)
                            ev = b.mark("dve", nc.vector.tensor_scalar(
                                ffin[:, ochunk, 0:TL], ps[:, 0:TL],
                                b1a[:, ochunk:ochunk + 1], None, ALU.add))
                            bank_ev[i] = ev
                        else:
                            b.wait("dve", mm)
                            ma = b.fence("dve", nc.vector.tensor_scalar(
                                ps[:, 0:TL], ps[:, 0:TL],
                                b1g[:, ochunk:ochunk + 1], None, ALU.add))
                            gi_ = rr % 2
                            gbuf = ggb[gi_]
                            b.wait("act", ma)
                            if gg_ev[gi_] is not None:
                                b.wait("act", gg_ev[gi_])
                            ga = b.mark("act", nc.scalar.activation(
                                gbuf[:, 0:TL], ps[:, 0:TL], AF.Gelu))
                            bank_ev[i] = ga
                            b.wait("pool", ga)
                            b.wait("pool", b.ms("dve"))
                            gp = b.mark("pool", nc.gpsimd.tensor_mul(
                                ffin[:, ochunk, 0:TL], ffin[:, ochunk, 0:TL],
                                gbuf[:, 0:TL]))
                            gg_ev[gi_] = gp
                    wb_ev[("w1", bi)] = b.ms("pe")
            if t == 0 and tag + "_ffin" in self.dbg_names:
                b.barrier(("pool", "dve", "pe", "act"))
                self.dbg_dump(tag + "_ffin", ffin.ap().rearrange("p c n -> p (c n)"),
                              [P, 22 * TL], dep=b.ms("pool"))
            b.wait("pe", b.ms("pool"))
            for oc in range(CCH):
                bi = oc % 2
                buf = w2b[bi]
                if wb_ev.get(("w2", bi)) is not None:
                    b.wait("pool", wb_ev[("w2", bi)])
                wld = b.mark("pool", nc.gpsimd.dma_start(
                    out=buf.ap(), in_=w2d[:, oc * P:(oc + 1) * P]
                    .rearrange("(k p) n -> p k n", p=P)), 16)
                b.wait("pe", wld)
                i = rr % 3
                rr += 1
                ps = psR[i]
                if bank_ev[i] is not None:
                    b.wait("pe", bank_ev[i])
                for kc in range(22):
                    mm = nc.tensor.matmul(ps[:, 0:TL], buf[:, kc, :],
                                          ffin[:, kc, 0:TL],
                                          start=(kc == 0), stop=(kc == 21))
                mm = b.mark("pe", mm)
                wb_ev[("w2", bi)] = (mm[0], mm[1])
                b.wait("dve", mm)
                ev = b.mark("dve", nc.vector.scalar_tensor_tensor(
                    stream.ap()[:, oc, c0:c0 + TL], ps[:, 0:TL],
                    b2c[:, oc:oc + 1],
                    stream.ap()[:, oc, c0:c0 + TL], ALU.add, ALU.add))
                bank_ev[i] = ev
        b.barrier()
        self.sbuf_free_from("f_w1a")
        self._ff_l = None

    def emit_ffn_layer(self, l, xsrc, xcol0, dst, dcol0, rot, tag, xch,
                       dst_ap=None, col0_ms=None):
        nc, b = self.nc, self.b
        if dst_ap is not None:
            class _W:
                def __init__(self, a): self._a = a
                def ap(self): return self._a
            dst = _W(dst_ap)
        hgate = self.sb["hgate"]
        hginv = self.sb["hginv"]
        qa = self.sbuf("f_qa", [65, FH, 2 * 512], BF16)
        ka = self.sbuf("f_ka", [65, FH, 4 * 512], BF16)
        va = self.sbuf("f_va", [P, 16, 512], BF16)
        o_fm = self.sbuf("f_ofm", [P, 4, 512], BF16)
        b.mark("pool", nc.gpsimd.memset(
            ka[0:64, :, :].rearrange("p h n -> p (h n)"), 0.0))
        b.mark("pool", nc.gpsimd.memset(
            va.ap().rearrange("p g n -> p (g n)"), 0.0))
        for h in range(FH):
            b.mark("pool", nc.gpsimd.memset(ka[64:65, h, :], 1.0))
            b.mark("pool", nc.gpsimd.memset(qa[64:65, h, :], 0.0))
        b.barrier(("pool", "pe", "dve", "act"))

        KS = {2: 0, 3: 1, 0: 3, 1: 1}
        KSP = {3: 0, 0: 2, 1: 3, 2: 1}
        VB = {2: 0, 3: 4, 0: 12, 1: 4}
        VBP = {3: 0, 0: 8, 1: 12, 2: 4}
        QB = {2: 0, 3: 512, 0: 512, 1: 512}

        def qkv(w):
            self.emit_qkv(l, xsrc, xcol0 + w * 512, w * 4, 4, qa, ka, va,
                          KS[w] * 512, VB[w], rot, 0, True, tag, qcol0=QB[w])
            b.barrier(("pe", "dve", "act", "pool"))

        def attn_wo(w):
            self.emit_attention(qa, ka, va, w, w == 0, o_fm, tag,
                                slots=(KSP[w], KS[w], VBP[w], VB[w]),
                                qbase=QB[w])
            b.barrier(("pe", "dve", "act", "pool"))
            self.emit_wo_resid(l, o_fm, xsrc, xcol0 + w * 512, dst,
                               dcol0 + w * 512, tag)
            b.barrier(("pe", "dve", "act", "pool"))

        qkv(2)
        qkv(3)
        mg = b.fence("dve", nc.vector.tensor_scalar(
            ka[0:64, :, 1024:1536], ka[0:64, :, 512:1024],
            hginv[0:64, :], None, ALU.mult))
        mg = b.fence("dve", nc.vector.tensor_scalar(
            va[:, 8:12, :], va[:, 4:8, :], hginv[:], None, ALU.mult))
        import os
        XKV = os.environ.get("XCH_KV", "1") == "1"
        XPA = os.environ.get("XCH_PA", "1") == "1"
        b.wait("sp", mg)
        mk = b.mark("sp", nc.sync.dma_start(
            out=xch["kb_i"].rearrange("p (h n) -> p h n", h=FH),
            in_=ka[0:64, :, 1024:1536]), 16)
        mv = b.mark("sp", nc.sync.dma_start(
            out=xch["vb_i"].rearrange("p (g n) -> p g n", g=4),
            in_=va[:, 8:12, :]), 16)
        b.wait("pool", mk, mv)
        if XKV:
            b.mark("cc", nc.gpsimd.collective_compute(
                "AllReduce", mybir.AluOpType.add,
                replica_groups=[[0, 1], [2, 3], [4, 5], [6, 7]],
                ins=[xch["kb_i"].opt()], outs=[xch["kb_o"].opt()]), 1)
            mcc_kv = b.mark("cc", nc.gpsimd.collective_compute(
                "AllReduce", mybir.AluOpType.add,
                replica_groups=[[0, 1], [2, 3], [4, 5], [6, 7]],
                ins=[xch["vb_i"].opt()], outs=[xch["vb_o"].opt()]), 1)
        else:
            mcc_kv = None
        attn_wo(3)
        pa_col = dcol0 + 2047
        pst = self.sbuf("f_past", [P, CCH], BF16)
        mpa = b.fence("dve", nc.vector.tensor_scalar(
            pst.ap().rearrange("p (c o) -> p c o", o=1),
            dst.ap()[:, :, pa_col:pa_col + 1], hginv[:], None, ALU.mult))
        b.wait("sp", mpa)
        mpd = b.mark("sp", nc.sync.dma_start(out=xch["pa_i"], in_=pst.ap()), 16)
        b.wait("pool", mpd)
        if XPA:
            mcc_pa = b.mark("cc", nc.gpsimd.collective_compute(
                "AllReduce", mybir.AluOpType.add,
                replica_groups=[[0, 1], [2, 3], [4, 5], [6, 7]],
                ins=[xch["pa_i"].opt()], outs=[xch["pa_o"].opt()]), 1)
        else:
            mcc_pa = None
        if col0_ms is not None:
            b.wait("pe", col0_ms)
        qkv(0)
        if XKV:
            b.wait("pool", mcc_kv)
            mlk = b.mark("pool", nc.gpsimd.dma_start(
                out=ka[0:64, :, 1024:1536],
                in_=xch["kb_o"].rearrange("p (h n) -> p h n", h=FH)), 16)
            mlv = b.mark("pool", nc.gpsimd.dma_start(
                out=va[:, 8:12, :],
                in_=xch["vb_o"].rearrange("p (g n) -> p g n", g=4)), 16)
            b.wait("dve", mlk, mlv)
            nc.vector.tensor_scalar(
                ka[0:64, :, 1024:1536], ka[0:64, :, 1024:1536],
                hgate[0:64, :], None, ALU.mult)
            mgk = b.fence("dve", nc.vector.tensor_scalar(
                va[:, 8:12, :], va[:, 8:12, :], hgate[:], None, ALU.mult))
            b.wait("pe", mgk)
        attn_wo(0)
        qkv(1)
        attn_wo(1)
        attn_wo(2)
        self.dbg_dump(tag + "_qa2", qa.ap().rearrange("p h n -> p (h n)"),
                      [65, FH * 1024], dep=b.ms("dve"))
        b.barrier(("pool", "sp", "pe", "dve", "act"))
        self.sbuf_free_from("f_qa")
        self._attn_consts = False
        self._wo_l = None
        self._qkv_l = None
        if os.environ.get("FF_SKIP", "0") != "1":
            self.emit_ff(l, dst, dcol0, T, tag,
                         stash_dram=xch["pa_o"] if XPA else None,
                         stash_ms=mcc_pa)


    def emit_final(self, x2v, out_dram):
        nc, b = self.nc, self.b
        g2g = [self.sbuf("fin_g2", [P, CCH, P], BF16),
               self.sbuf("fin_g2b", [P, CCH, P], BF16)]
        g2t = [self.sbuf("fin_g2t", [P, C], BF16),
               self.sbuf("fin_g2tb", [P, C], BF16)]
        xg = [self.sbuf("fin_x", [P, C], F32), self.sbuf("fin_xb", [P, C], F32)]
        sg = [self.sbuf("fin_s", [P, C], F32), self.sbuf("fin_sb", [P, C], F32)]
        og = [self.sbuf("fin_o", [P, C], F32), self.sbuf("fin_ob", [P, C], F32)]
        s_sc = self.dram_tmp("s_sc", [T, C], F32)
        xmd = self.dram_in("xm", [NGH * P, C])
        B = self.banks()
        pstr = B[7]
        ident = self.sb["ident_bf"]
        ld_ev = [None, None]
        sil_ev = [None, None]
        b7_ev = None
        mo_ev = [None, None]
        for g in range(NG):
            par = g % 2
            if mo_ev[par] is not None:
                b.wait("sp", mo_ev[par])
            lx = b.mark("sp", nc.sync.dma_start(
                out=xg[par][:], in_=xmd[(g + 1) * P:(g + 2) * P, :]), 16)
            ls = b.mark("sp", nc.sync.dma_start(
                out=sg[par][:], in_=s_sc[g * P:(g + 1) * P, :]), 16)
            if sil_ev[par] is not None:
                b.wait("act", sil_ev[par])
            elif g == 0:
                b.wait("act", b.ms("pe"))
            ma = b.mark("act", nc.scalar.activation(
                g2g[par].ap(), x2v[:, :, 1 + g * P:1 + (g + 1) * P], AF.Silu))
            b.wait("pe", ma)
            if b7_ev is not None:
                b.wait("pe", b7_ev)
            mt = None
            for cc in range(CCH):
                mt = nc.tensor.transpose(pstr[:, cc * P:(cc + 1) * P],
                                         g2g[par][:, cc, :], ident[:])
            mt = b.mark("pe", mt)
            sil_ev[par] = mt
            b.wait("dve", mt)
            b7_ev = b.mark("dve", nc.vector.tensor_copy(
                g2t[par].ap().rearrange("p (c n) -> p c n", c=CCH),
                pstr.ap().rearrange("p (c n) -> p c n", c=CCH)))
            b.wait("dve", lx, ls)
            if ld_ev[par] is not None:
                b.wait("dve", ld_ev[par])
            nc.vector.tensor_mul(og[par][:], sg[par][:], g2t[par][:])
            mo = b.fence("dve", nc.vector.tensor_add(
                og[par][:], og[par][:], xg[par][:]))
            mo_ev[par] = mo
            b.wait("sp", mo)
            ld_ev[par] = b.mark("sp", nc.sync.dma_start(
                out=out_dram[g * P:(g + 1) * P, :], in_=og[par][:]), 16)
        b.barrier()

def build(dbg=(), stage_limit=99):
    import os
    stage_limit = int(os.environ.get("STAGE_LIMIT", stage_limit))
    k = K(dbg)
    nc, b = k.nc, k.b

    xm = k.dram_in("xm", [NGH * P, C])
    ident = k.dram_in("ident", [P, P])
    ones2 = k.dram_in("ones2", [P, 2])
    scanmask = k.dram_in("scan_mask", [P, P], U8)
    u_fm = k.dram_in("u_fm", [P, CCH])

    k.sbuf("ident_bf", [P, P], BF16)
    k.sbuf("ones2_bf", [P, 2], BF16)
    k.sbuf("scanmask_sb", [P, P], U8)
    k.sbuf("u_sb", [P, CCH], F32)
    hgate = k.sbuf("hgate", [P, 1], F32)
    hginv = k.sbuf("hginv", [P, 1], F32)
    xn_fm = k.sbuf("xn_fm", [P, CCH, NGH * P], BF16)
    S_init = k.sbuf("S_init", [P, CCH, 64], F32)
    out_dram = k.dram_out("out", [T, C], F32)

    for nm, src_ in (("ident_bf", ident), ("ones2_bf", ones2),
                     ("scanmask_sb", scanmask), ("u_sb", u_fm)):
        b.mark("pool", nc.gpsimd.dma_start(out=k.sb[nm].ap(), in_=src_), 16)
    b.mark("pool", nc.gpsimd.dma_start(
        out=hgate.ap(), in_=k.dram_in("halo_gate", [P, 1])), 16)
    b.mark("pool", nc.gpsimd.dma_start(
        out=hginv.ap(), in_=k.dram_in("halo_gate_inv", [P, 1])), 16)
    b.mark("pool", nc.gpsimd.memset(S_init.ap(), 0.0))
    k.constv(1e-5)
    k.constv(1e-30)
    b.barrier(("pool", "pe", "dve", "act", "sp"))

    if stage_limit >= 5:
        k.emit_ln_fm(xm, xn_fm.ap(), NGH)
        b.barrier()
    k.dbg_dump("xn_fm", xn_fm.ap().rearrange("p c n -> p (c n)"),
               [P, CCH * NGH * P], dep=k.b.ms("dve"))
    if stage_limit >= 6:
        m2a, td2 = k.emit_rwkv_mix(xn_fm, "wkvrg", T, "main")
    if stage_limit >= 7:
        k.emit_rwkv_scan("main", T, m2a, td2, S_init, with_y=True,
                         save_corr=True)
    if stage_limit >= 8:
        b.barrier()
        sfin_b = k.dram_tmp("sfin_b", [P, CCH * 64], F32)
        sfin_g = k.dram_tmp("sfin_g", [P, CCH * 64], F32)
        arg_sb = k.sb["sc_arg"]
        S_ex = arg_sb.ap()[:, 512:1024].rearrange("p (c n) -> p c n", c=CCH)
        expb = arg_sb[:, 256:384]
        bst_tmp = arg_sb[:, 384:512]
        sfin_stage = arg_sb.ap()[:, 1024:1536].rearrange(
            "p (c n) -> p c n", c=CCH)
        mgt = b.fence("dve", nc.vector.tensor_scalar(
            sfin_stage, S_init.ap(), hginv[:], None, ALU.mult))
        b.wait("sp", mgt)
        mS = b.mark("sp", nc.sync.dma_start(
            out=sfin_b, in_=arg_sb[:, 1024:1536]), 16)
        b.wait("pool", mS)
        mcc = b.mark("cc", nc.gpsimd.collective_compute(
            "AllReduce", mybir.AluOpType.add,
            replica_groups=[[0, 1], [2, 3], [4, 5], [6, 7]],
            ins=[sfin_b.opt()], outs=[sfin_g.opt()]), 1)
        b.wait("pool", mcc)
        mL = b.mark("pool", nc.gpsimd.dma_start(
            out=S_ex, in_=sfin_g.rearrange("p (c n) -> p c n", c=CCH)), 16)
        mB = b.mark("pool", nc.gpsimd.dma_start(
            out=bst_tmp, in_=k.tmp["bst_sc"]), 16)
        b.wait("dve", mL)
        b.fence("dve", nc.vector.tensor_scalar(
            S_ex, S_ex, hgate[:], None, ALU.mult))
        b.wait("act", b.ms("dve"), mB)
        b.mark("act", nc.scalar.activation(expb, bst_tmp, AF.Exp))
        b.barrier(("act", "dve", "pe", "sp", "pool"))
        k.emit_rwkv_out(corr=True)

    if stage_limit >= 9:
        b.barrier()
        k.sbuf_free_from("wsmall")
        rot = [k.sbuf(f"f_rot{i}", [P, NG, FD], BF16) for i in range(4)]
        for i in range(4):
            b.mark("pool", nc.gpsimd.dma_start(
                out=rot[i].ap(),
                in_=k.dram_in("rot_main", [4, T, FD])[i]
                .rearrange("(g p) n -> p g n", p=P)), 16)
        x1_fm = k.sbuf("x1_fm", [P, CCH, T + 1], BF16)
        b.barrier(("pool", "pe", "dve", "act"))

        def mkxch(pfx):
            return dict(
                kb_i=k.dram_tmp(pfx + "kb_i", [64, FH * 512], BF16),
                kb_o=k.dram_tmp(pfx + "kb_o", [64, FH * 512], BF16),
                vb_i=k.dram_tmp(pfx + "vb_i", [P, 4 * 512], BF16),
                vb_o=k.dram_tmp(pfx + "vb_o", [P, 4 * 512], BF16),
                pa_i=k.dram_tmp(pfx + "pa_i", [P, CCH], BF16),
                pa_o=k.dram_tmp(pfx + "pa_o", [P, CCH], BF16))

        k.emit_ffn_layer(0, xn_fm, P, x1_fm, 1, rot, "L1", mkxch("l1"))
        k.dbg_dump("x1_fm", x1_fm.ap().rearrange("p c n -> p (c n)"),
                   [P, CCH * (T + 1)], dep=k.b.ms("dve"))
    if stage_limit >= 10:
        x2v = xn_fm.ap()[:, :, 0:T + 1]
        b.barrier()
        x1c_i = k.dram_tmp("x1c_i", [P, CCH], BF16)
        x1c_o = k.dram_tmp("x1c_o", [P, CCH], BF16)
        x1cst = k.sbuf("x1c_stage", [P, CCH], BF16)
        mg = b.fence("dve", nc.vector.tensor_scalar(
            x1cst.ap().rearrange("p (c o) -> p c o", o=1),
            x1_fm.ap()[:, :, T:T + 1], hginv[:], None, ALU.mult))
        b.wait("sp", mg)
        mxd = b.mark("sp", nc.sync.dma_start(out=x1c_i, in_=x1cst.ap()), 16)
        b.wait("pool", mxd)
        import os as _os
        if _os.environ.get("XCH_X1C", "1") == "1":
            mcc_x1 = b.mark("cc", nc.gpsimd.collective_compute(
                "AllReduce", mybir.AluOpType.add,
                replica_groups=[[0, 1], [2, 3], [4, 5], [6, 7]],
                ins=[x1c_i.opt()], outs=[x1c_o.opt()]), 1)
            b.wait("pool", mcc_x1)
        mxl = b.mark("pool", nc.gpsimd.dma_start(
            out=x1_fm.ap()[:, :, 0:1],
            in_=x1c_o.rearrange("p (c o) -> p c o", o=1),
            allow_slow_non_contiguous=True), 16)
        b.wait("dve", mxl)
        mcol = b.fence("dve", nc.vector.tensor_scalar(
            x1_fm.ap()[:, :, 0:1], x1_fm.ap()[:, :, 0:1],
            hgate[:], None, ALU.mult))
        k.emit_ffn_layer(1, x1_fm, 1, None, 1, rot, "L2", mkxch("l2"),
                         dst_ap=x2v, col0_ms=mcol)
        k.dbg_dump("x2_fm", x2v, [P, CCH, T + 1], dep=k.b.ms("dve"))

    if stage_limit >= 11:
        b.barrier()
        k.sbuf_free_from("f_qa")
        k.emit_final(xn_fm.ap()[:, :, 0:T + 1], out_dram)

    if "s_sc" in k.dbg_names and "s_sc" in k.tmp:
        o = k.dram_out("dbg_s_sc", [T, C], F32)
        b.mark("sp", nc.sync.dma_start(out=o, in_=k.tmp["s_sc"]), 16)
    if "y_sc" in k.dbg_names and "y_sc" in k.tmp:
        o = k.dram_out("dbg_y_sc", [T, C], F32)
        b.mark("pool", nc.gpsimd.dma_start(
            out=o.rearrange("(g p) n -> g p n", p=P), in_=k.tmp["y_sc"]), 16)
    if "k_sc" in k.dbg_names and "k_sc" in k.tmp:
        o = k.dram_out("dbg_k_sc", [CCH, P, T], F32)
        b.mark("pool", nc.gpsimd.dma_start(out=o, in_=k.tmp["k_sc"]), 16)
    if "v_sc" in k.dbg_names and "v_sc" in k.tmp:
        o = k.dram_out("dbg_v_sc", [T // P, P, C], F32)
        b.mark("pool", nc.gpsimd.dma_start(out=o, in_=k.tmp["v_sc"]), 16)
    if "g_sc" in k.dbg_names and "g_sc" in k.tmp:
        o = k.dram_out("dbg_g_sc", [T // P, P, C], F32)
        b.mark("pool", nc.gpsimd.dma_start(out=o, in_=k.tmp["g_sc"]), 16)

    b.barrier(("sp", "pool"))
    return k


def kernel(**inputs):
    cores = list(range(8))
    in_maps = [make_core_inputs(inputs, c) for c in cores]
    k = build()
    used = set(k.inp.keys())
    in_maps = [{n: m[n] for n in used} for m in in_maps]
    res = run_bass_kernel_spmd(k.nc, in_maps, core_ids=cores)
    x = np.asarray(inputs["L0_attn_x"])
    out = np.zeros_like(x)
    for c in cores:
        b, half = c // 2, c % 2
        out[b, half * T:(half + 1) * T] = res.results[c]["out"]
    return out
